# version 06
# baseline (speedup 1.0000x reference)
import numpy as np
import ml_dtypes
from contextlib import ExitStack

import concourse.bass as bass
import concourse.mybir as mybir
from concourse.bass_utils import run_bass_kernel_spmd

F32 = mybir.dt.float32
BF16 = mybir.dt.bfloat16
AF = mybir.ActivationFunctionType
ALU = mybir.AluOpType
AX = mybir.AxisListType

D = 2048
NCORE = 8
HEADS = 32
HS = 64
L = 32
CH = 512
NE = 32
DI = 768
EPS = 1e-5
ENGINES = ("te", "ve", "se", "pe", "sy")
G_SKIP = ""


class Emitter:

    def __init__(self):
        self.plan = []
        self.last_w = {}
        self.readers = {}
        self.batch_open = {}
        self.batch_count = {}
        self.batch_sizes = {}
        self.cur_batch_size = {}
        self.barrier_deps = []

    def _record(self, rec, reads, writes, me):
        deps = list(self.barrier_deps)
        for r in reads:
            w = self.last_w.get(r)
            if w is not None:
                deps.append(w)
        for w in writes:
            lw = self.last_w.get(w)
            if lw is not None:
                deps.append(lw)
            for src in self.readers.get(w, {}).values():
                deps.append(src)
        rec["deps"] = deps
        self.plan.append(rec)
        for r in reads:
            self.readers.setdefault(r, {})[me[0:2] if me[0] == "dma" else ("e", me[1])] = me
        for w in writes:
            self.last_w[w] = me
            self.readers[w] = {}

    def op(self, eng, fn, reads=(), writes=(), tag=""):
        oid = len(self.plan)
        rec = dict(eng=eng, fn=fn, is_dma=False, stream=None, need_inc=False,
                   oid=oid, tag=tag)
        self._record(rec, reads, writes, ("eng", eng, oid))
        return oid

    def dma(self, eng, fn, stream, reads=(), writes=(), tag=""):
        oid = len(self.plan)
        extra = []
        if not self.batch_open.get(stream, False):
            self.batch_open[stream] = True
            self.cur_batch_size[stream] = 0
            nb = self.batch_count.get(stream, 0)
            if nb > 0:
                extra.append(("dma", stream, nb - 1))
        self.cur_batch_size[stream] += 1
        batch_idx = self.batch_count.get(stream, 0)
        rec = dict(eng=eng, fn=fn, is_dma=True, stream=stream, batch=batch_idx,
                   need_inc=False, oid=oid, tag=tag)
        save = self.barrier_deps
        self.barrier_deps = save + extra
        self._record(rec, reads, writes, ("dma", stream, batch_idx))
        self.barrier_deps = save
        rec["deps"] = [d for d in rec["deps"]
                       if not (d[0] == "dma" and d[1] == stream and d not in extra)]
        return oid

    def close_batch(self, stream):
        if self.batch_open.get(stream, False):
            self.batch_open[stream] = False
            self.batch_sizes.setdefault(stream, []).append(self.cur_batch_size[stream])
            self.batch_count[stream] = self.batch_count.get(stream, 0) + 1

    def barrier(self):
        for s in list(self.batch_open):
            self.close_batch(s)
        deps = []
        last_by_eng = {}
        for rec in self.plan:
            if not rec["is_dma"]:
                last_by_eng[rec["eng"]] = rec["oid"]
        for e, oid in last_by_eng.items():
            deps.append(("eng", e, oid))
        for s, nb in self.batch_count.items():
            if nb > 0:
                deps.append(("dma", s, nb - 1))
        self.barrier_deps = deps

    def emit(self, nc, stack, final_stream=()):
        for s in list(self.batch_open):
            self.close_batch(s)
        for rec in self.plan:
            for dep in rec["deps"]:
                if dep[0] == "eng":
                    _, peng, poid = dep
                    if peng == rec["eng"] and peng == "te":
                        continue
                    self.plan[poid]["need_inc"] = True
        inc_val = {}
        cnt = {e: 0 for e in ENGINES}
        for rec in self.plan:
            if not rec["is_dma"] and rec["need_inc"]:
                cnt[rec["eng"]] += 1
                inc_val[rec["oid"]] = cnt[rec["eng"]]
        cum = {}
        for s, sizes in self.batch_sizes.items():
            c, lst = 0, []
            for sz in sizes:
                c += sz
                lst.append(c)
            cum[s] = lst
        esem = {e: stack.enter_context(nc.semaphore(f"es_{e}")) for e in ENGINES}
        dsem = {s: stack.enter_context(nc.semaphore(f"ds_{s}")) for s in self.batch_sizes}

        def dep_to_wait(dep, myeng):
            if dep[0] == "eng":
                _, peng, poid = dep
                if peng == myeng and peng == "te":
                    return None
                return (("e", peng), inc_val[poid])
            _, s, b = dep
            return (("d", s), 16 * cum[s][b])

        per_eng = {e: [] for e in ENGINES}
        for rec in self.plan:
            per_eng[rec["eng"]].append(rec)
        nwaits = [0]

        def make_section(eng, recs):
            def section(engobj):
                wm = {}
                for rec in recs:
                    for dep in rec["deps"]:
                        w = dep_to_wait(dep, eng)
                        if w is None:
                            continue
                        key, val = w
                        if wm.get(key, 0) >= val:
                            continue
                        engobj.wait_ge(esem[key[1]] if key[0] == "e" else dsem[key[1]], val)
                        wm[key] = val
                        nwaits[0] += 1
                    inst = rec["fn"](engobj)
                    if rec["is_dma"]:
                        inst.then_inc(dsem[rec["stream"]], 16)
                    elif rec["need_inc"]:
                        inst.then_inc(esem[eng], 1)
                if eng == "sy" and final_stream:
                    for fs_ in final_stream:
                        if fs_ in cum:
                            engobj.wait_ge(dsem[fs_], 16 * cum[fs_][-1])
            return section

        blk = stack.enter_context(nc.Block())
        blk.tensor(make_section("te", per_eng["te"]))
        blk.vector(make_section("ve", per_eng["ve"]))
        blk.scalar(make_section("se", per_eng["se"]))
        blk.gpsimd(make_section("pe", per_eng["pe"]))
        blk.sync(make_section("sy", per_eng["sy"]))
        self.stats = dict(n_ops=len(self.plan), n_waits=nwaits[0],
                          n_incs=sum(cnt.values()))



def build_program(T=2048, phases="ABCDEFGHIJK", debug=()):
    nc = bass.Bass("TRN2", target_bir_lowering=False, debug=False)
    em = Emitter()
    NT = T // 128
    NCH = T // CH
    NM = T // L
    MPC = CH // L
    NPAIR = HEADS // 2
    TB = min(1024, T)
    NB = T // TB
    NTTB = TB // 128
    NCHB = TB // CH
    CCW = NPAIR * HS + 16

    ap = lambda name, shape, kind: nc.dram_tensor(name, shape, F32, kind=kind).ap()
    x_in = ap("x", [T, D], "ExternalInput")
    Wx = ap("Wx", [D, D], "ExternalInput")
    Ww = ap("Ww", [D, D], "ExternalInput")
    bw = ap("bw", [128, 16], "ExternalInput")
    Wk = ap("Wk", [D, D], "ExternalInput")
    Wv = ap("Wv", [D, D], "ExternalInput")
    Wr = ap("Wr", [D, D], "ExternalInput")
    Wo = ap("Wo", [D, D], "ExternalInput")
    gamma = ap("gamma", [128, 16], "ExternalInput")
    beta = ap("beta", [128, 16], "ExternalInput")
    rweights = ap("rw", [NE, D], "ExternalInput")
    wgate = ap("wg", [NE, D, DI], "ExternalInput")
    wup = ap("wu", [NE, D, DI], "ExternalInput")
    wdown = ap("wdn", [NE, DI, D], "ExternalInput")
    ident_c = ap("identity", [128, 128], "ExternalInput")
    mrepb_c = ap("mrepb_c", [128, 512], "ExternalInput")
    mseg_c = ap("mseg_c", [128, CH], "ExternalInput")
    gmask_c = ap("gmask_c", [128, NCORE], "ExternalInput")
    out_o = ap("out", [T, D], "ExternalOutput")

    out_names = ["out"]
    dbg = {}
    for name, shape in debug:
        dbg[name] = nc.dram_tensor("dbg_" + name, shape, F32, kind="ExternalOutput").ap()
        out_names.append("dbg_" + name)

    sc = lambda name, shape, dt: nc.dram_tensor(name, shape, dt).ap()
    q_s = sc("q_s", [D, T], BF16)
    ek_s = sc("ek_s", [D, T], F32)
    er_s = sc("er_s", [D, T], F32)
    ktf_s = sc("ktf_s", [D, T], BF16)
    ktt_s = sc("ktt_s", [T, D], BF16)
    vtt_s = sc("vtt_s", [T, D], BF16)
    rtf_s = sc("rtf_s", [D, T], BF16)
    rrf_s = sc("rrf_s", [D, T], BF16)
    y_s = sc("y_s", [D, T], BF16)
    h1_s = sc("h1_s", [T, D], F32)
    hf_s = sc("hf_s", [D, T], BF16)
    cc_in = nc.dram_tensor("cc_in", [128, CCW], F32)
    cc_out = nc.dram_tensor("cc_out", [NCORE * 128, CCW], F32)

    stack = ExitStack()
    sb = lambda name, shape, dt=F32: stack.enter_context(nc.sbuf_tensor(name, shape, dt))

    gamf = sb("gamf", [128, 16]); betf = sb("betf", [128, 16]); bwf = sb("bwf", [128, 16])
    idbf = sb("idbf", [128, 128], BF16)
    idf32 = sb("idf32", [128, 128])
    mrepb = sb("mrepb", [128, 512])
    mseg = sb("mseg", [128, CH])
    gmask = sb("gmask", [128, NCORE])
    eTC = sb("eTC", [128, 16 * NM]); eLs = sb("eLs", [128, 16 * NM])
    ltcc = sb("ltcc", [128, 16]); DsumB = sb("DsumB", [128, 16])
    SpackB = sb("SpackB", [128, NPAIR * HS]); SinB = sb("SinB", [128, NPAIR * HS])
    SinBF = sb("SinBF", [128, NPAIR * HS], BF16)
    st8 = sb("st8", [128, 16])

    ps = [stack.enter_context(nc.psum_tensor(f"ps{i}", [128, 512], F32)) for i in range(6)]
    pt = [stack.enter_context(nc.psum_tensor(f"pt{i}", [128, 1024], BF16)) for i in range(2)]

    em.dma("sy", lambda e: e.dma_start(out=gamf[:], in_=gamma), "cst", writes=[("gamf",)])
    em.dma("sy", lambda e: e.dma_start(out=betf[:], in_=beta), "cst", writes=[("betf",)])
    em.dma("sy", lambda e: e.dma_start(out=bwf[:], in_=bw), "cst", writes=[("bwf",)])
    em.dma("sy", lambda e: e.dma_start(out=idf32[:], in_=ident_c), "cst", writes=[("idf32",)])
    em.dma("sy", lambda e: e.dma_start(out=mrepb[:], in_=mrepb_c), "cst", writes=[("mrepb",)])
    em.dma("sy", lambda e: e.dma_start(out=mseg[:], in_=mseg_c), "cst", writes=[("mseg",)])
    em.dma("sy", lambda e: e.dma_start(out=gmask[:], in_=gmask_c), "cst", writes=[("gmask",)])
    em.close_batch("cst")
    em.op("ve", lambda v: v.tensor_copy(idbf[:], idf32[:]), reads=[("idf32",)], writes=[("idbf",)])

    def ln_tile(src, srckey, dst, dstkey, extra_reads=()):
        em.op("se", lambda s_: s_.activation(lnsq[:], src[:, 0:D], AF.Square,
                                             accum_out=st8[:, 1:2]),
              reads=[srckey] + list(extra_reads), writes=[("lnsq",), ("st8", 1)])
        em.op("ve", lambda v: v.reduce_sum(st8[:, 0:1], src[:, 0:D], axis=AX.X),
              reads=[srckey], writes=[("st8", 0)])
        em.op("ve", lambda v: v.tensor_scalar_mul(st8[:, 0:1], st8[:, 0:1], 1.0 / D),
              reads=[("st8", 0)], writes=[("st8", 0)])
        em.op("ve", lambda v: v.tensor_scalar(st8[:, 2:3], st8[:, 0:1], st8[:, 0:1],
                                              -1.0, ALU.mult, ALU.mult),
              reads=[("st8", 0)], writes=[("st8", 2)])
        em.op("ve", lambda v: v.scalar_tensor_tensor(st8[:, 1:2], st8[:, 1:2], 1.0 / D,
                                                     st8[:, 2:3], ALU.mult, ALU.add),
              reads=[("st8", 1), ("st8", 2)], writes=[("st8", 1)])
        em.op("ve", lambda v: v.tensor_scalar_add(st8[:, 1:2], st8[:, 1:2], float(EPS)),
              reads=[("st8", 1)], writes=[("st8", 1)])
        em.op("se", lambda s_: s_.activation(st8[:, 1:2], st8[:, 1:2], AF.Ln),
              reads=[("st8", 1)], writes=[("st8", 1)])
        em.op("se", lambda s_: s_.activation(st8[:, 1:2], st8[:, 1:2], AF.Exp, scale=-0.5),
              reads=[("st8", 1)], writes=[("st8", 1)])
        em.op("ve", lambda v: v.tensor_scalar(st8[:, 3:4], st8[:, 0:1], st8[:, 1:2],
                                              -1.0, ALU.mult, ALU.mult),
              reads=[("st8", 0), ("st8", 1)], writes=[("st8", 3)])
        em.op("se", lambda s_: s_.activation(dst[:, 0:D], src[:, 0:D], AF.Identity,
                                             bias=st8[:, 3:4], scale=st8[:, 1:2]),
              reads=[srckey, ("st8", 1), ("st8", 3)], writes=[dstkey])

    poolA = ExitStack()
    sbA = lambda name, shape, dt=F32: poolA.enter_context(nc.sbuf_tensor(name, shape, dt))
    xnf = sbA("xnf", [128, 16 * T], BF16)
    wblk = [sbA(f"wblk{i}", [128, 16 * 512], BF16) for i in range(2)]
    xt = [sbA(f"xt{i}", [128, D]) for i in range(2)]
    lnsq = sbA("lnsq", [128, D])
    lnt2 = sbA("lnt2", [128, D])
    xnbf = sbA("xnbf", [128, D], BF16)
    zt = [sbA(f"zt{i}", [128, CH]) for i in range(2)]
    spt = [sbA(f"spt{i}", [128, CH]) for i in range(4)]
    ekt = [sbA("ekt0", [128, CH])]
    ert = [sbA("ert0", [128, CH])]
    kst = [sbA(f"kst{i}", [128, CH], BF16) for i in range(2)]
    asm = [sbA(f"asm{i}", [128, D], BF16) for i in range(4)]
    eLx = sbA("eLx", [128, CH])
    qc = sbA("qc", [128, 16 * CH], BF16)

    xnf_v = xnf[:].rearrange("p (ct t) -> p ct t", ct=16)
    wblk_v = lambda i: wblk[i][:].rearrange("p (ct f) -> p ct f", ct=16)

    if "A" in phases:
        for tt in range(NT):
            xb, xkey = xt[tt % 2], ("xt", tt % 2)
            em.dma("sy", lambda e, tt=tt, xb=xb: e.dma_start(
                out=xb[:], in_=x_in[tt * 128:(tt + 1) * 128, :]), "xin", writes=[xkey])
            em.close_batch("xin")
            ln_tile(xb, xkey, lnt2, ("lnt2",))
            ln_tile(lnt2, ("lnt2",), lnt2, ("lnt2",))
            em.op("ve", lambda v: v.tensor_copy(xnbf[:], lnt2[:]),
                  reads=[("lnt2",)], writes=[("xnbf",)])
            for ft in range(16):
                pb = ft % 2
                em.op("te", lambda t, ft=ft, pb=pb: t.transpose(
                    pt[pb][:, 0:128], xnbf[:, ft * 128:(ft + 1) * 128], idbf[:]),
                    reads=[("xnbf",), ("idbf",)], writes=[("pt", pb)])
                em.op("se", lambda s_, ft=ft, pb=pb, tt=tt: s_.activation(
                    xnf_v[:, ft, tt * 128:(tt + 1) * 128], pt[pb][:, 0:128],
                    AF.Identity, bias=betf[:, ft:ft + 1], scale=gamf[:, ft:ft + 1]),
                    reads=[("pt", pb), ("gamf",), ("betf",)], writes=[("xnf",)])
        if "xnf" in dbg:
            em.dma("pe", lambda e: e.dma_start(
                out=dbg["xnf"].rearrange("(ct p) t -> p ct t", p=128), in_=xnf_v),
                "dbgp", reads=[("xnf",)])
            em.close_batch("dbgp")

    if "B" in phases:
        for fb in range(4):
            ws = fb % 2
            em.dma("pe", lambda e, fb=fb, ws=ws: e.dma_start(
                out=wblk_v(ws), in_=Wx[:, fb * 512:(fb + 1) * 512].rearrange("(ct p) f -> p ct f", p=128)),
                "wgt", writes=[("wblk", ws)])
            em.close_batch("wgt")
            for fi in range(4):
                ft = fb * 4 + fi
                for tc in range(NCH):
                    bank = (ft * NCH + tc) % 2
                    for ct in range(16):
                        em.op("te", lambda t, ct=ct, fi=fi, tc=tc, bank=bank, ws=ws: t.matmul(
                            ps[bank][:, :], wblk_v(ws)[:, ct, fi * 128:(fi + 1) * 128],
                            xnf_v[:, ct, tc * CH:(tc + 1) * CH], start=(ct == 0), stop=(ct == 15)),
                            reads=[("wblk", ws), ("xnf",)], writes=[("psb", bank)])
                    slot = bank
                    em.op("se", lambda s_, bank=bank, slot=slot: s_.activation(
                        kst[slot][:], ps[bank][:, :], AF.Identity),
                        reads=[("psb", bank)], writes=[("kst", slot)])
                    em.dma("sy", lambda e, ft=ft, tc=tc, slot=slot: e.dma_start(
                        out=q_s[ft * 128:(ft + 1) * 128, tc * CH:(tc + 1) * CH], in_=kst[slot][:]),
                        "qst", reads=[("kst", slot)], writes=[("qS",)])
                    em.close_batch("qst")

    if "C" in phases:
        em.op("ve", lambda v: v.memset(ltcc[:], 0.0), writes=[("ltc",)])
    def emit_C_tc(tc):
        em.dma("sy", lambda e, tc=tc: e.dma_start(
            out=qc[:].rearrange("p (ct t) -> p ct t", ct=16),
            in_=q_s[:, tc * CH:(tc + 1) * CH].rearrange("(ct p) t -> p ct t", p=128)),
            "qld", reads=[("qS",)], writes=[("qc",)])
        em.close_batch("qld")
        for fb in range(4):
            ws = 0
            em.dma("pe", lambda e, fb=fb, ws=ws, tc=tc: e.dma_start(
                out=wblk_v(ws), in_=Ww[:, fb * 512:(fb + 1) * 512].rearrange("(ct p) f -> p ct f", p=128)),
                "wgt", writes=[("wblk", ws)])
            em.close_batch("wgt")
            for fi in range(4):
                ft = fb * 4 + fi
                bank = ft % 2
                for ct in range(16):
                    em.op("te", lambda t, ct=ct, fi=fi, bank=bank, ws=ws: t.matmul(
                        ps[bank][:, :], wblk_v(ws)[:, ct, fi * 128:(fi + 1) * 128],
                        qc[:].rearrange("p (ct t) -> p ct t", ct=16)[:, ct, :],
                        start=(ct == 0), stop=(ct == 15)),
                        reads=[("wblk", ws), ("qc",)], writes=[("psb", bank)])
                zb = zt[bank]
                em.op("se", lambda s_, bank=bank, ft=ft, zb=zb: s_.activation(
                    zb[:], ps[bank][:, :], AF.Identity, bias=bwf[:, ft:ft + 1]),
                    reads=[("psb", bank), ("bwf",)], writes=[("zt", bank)])
                em.op("se", lambda s_, zb=zb, bank=bank: s_.activation(spt[0][:], zb[:], AF.Abs),
                      reads=[("zt", bank)], writes=[("spt", 0)])
                em.op("se", lambda s_: s_.activation(spt[1][:], spt[0][:], AF.Exp, scale=-1.0),
                      reads=[("spt", 0)], writes=[("spt", 1)])
                em.op("se", lambda s_: s_.activation(spt[0][:], spt[1][:], AF.Ln, bias=1.0),
                      reads=[("spt", 1)], writes=[("spt", 0)])
                em.op("ve", lambda v, zb=zb, bank=bank: v.tensor_scalar_max(spt[2][:], zb[:], 0.0),
                      reads=[("zt", bank)], writes=[("spt", 2)])
                em.op("ve", lambda v: v.tensor_add(spt[3][:], spt[2][:], spt[0][:]),
                      reads=[("spt", 2), ("spt", 0)], writes=[("spt", 3)])
                em.op("ve", lambda v: v.tensor_tensor_scan(
                    spt[2][:, ::-1], mseg[:], spt[3][:, ::-1], 0.0, ALU.mult, ALU.add),
                    reads=[("spt", 3), ("mseg",)], writes=[("spt", 2)])
                em.op("ve", lambda v: v.tensor_sub(spt[2][:], spt[2][:], spt[3][:]),
                      reads=[("spt", 2), ("spt", 3)], writes=[("spt", 2)])
                es = 0
                em.op("se", lambda s_, es=es: s_.activation(ekt[es][:], spt[2][:], AF.Exp, scale=-1.0),
                      reads=[("spt", 2)], writes=[("ekt", es)])
                em.op("se", lambda s_, es=es: s_.activation(ert[es][:], spt[2][:], AF.Exp),
                      reads=[("spt", 2)], writes=[("ert", es)])
                em.dma("sy", lambda e, ft=ft, tc=tc, es=es: e.dma_start(
                    out=ek_s[ft * 128:(ft + 1) * 128, tc * CH:(tc + 1) * CH], in_=ekt[es][:]),
                    "ekst", reads=[("ekt", es)], writes=[("ekS",)])
                em.dma("sy", lambda e, ft=ft, tc=tc, es=es: e.dma_start(
                    out=er_s[ft * 128:(ft + 1) * 128, tc * CH:(tc + 1) * CH], in_=ert[es][:]),
                    "ekst", reads=[("ert", es)], writes=[("ekS",)])
                em.close_batch("ekst")
                c0 = ft * NM + tc * MPC
                em.op("ve", lambda v, c0=c0: v.tensor_tensor(
                    eTC[:, c0:c0 + MPC], spt[2][:, 0::L], spt[3][:, 0::L], ALU.add),
                    reads=[("spt", 2), ("spt", 3)], writes=[("eTC", c0)])
                em.op("ve", lambda v, c0=c0, ft=ft: v.tensor_tensor_scan(
                    eLs[:, c0:c0 + MPC], mseg[:, 1:1 + MPC], eTC[:, c0:c0 + MPC],
                    ltcc[:, ft:ft + 1], ALU.mult, ALU.add),
                    reads=[("eTC", c0), ("ltc",), ("mseg",)], writes=[("eL", c0)])
                em.op("ve", lambda v, c0=c0, ft=ft: v.tensor_copy(
                    ltcc[:, ft:ft + 1], eLs[:, c0 + MPC - 1:c0 + MPC]),
                    reads=[("eL", c0)], writes=[("ltc",)])
                em.op("se", lambda s_, c0=c0: s_.activation(
                    eTC[:, c0:c0 + MPC], eTC[:, c0:c0 + MPC], AF.Exp, scale=-1.0),
                    reads=[("eTC", c0)], writes=[("eTC", c0)])
                em.op("se", lambda s_, c0=c0: s_.activation(
                    eLs[:, c0:c0 + MPC], eLs[:, c0:c0 + MPC], AF.Exp, scale=-1.0),
                    reads=[("eL", c0)], writes=[("eL", c0)])

    def emit_E_ttg(ttg):
        for jb in range(4):
            ws = 1
            em.dma("pe", lambda e, jb=jb, ws=ws, ttg=ttg: e.dma_start(
                out=wblk_v(ws), in_=Wv[:, jb * 512:(jb + 1) * 512].rearrange("(ct p) f -> p ct f", p=128)),
                "wgt2", writes=[("wblk", ws)])
            em.close_batch("wgt2")
            for tl in range(4):
                tt = ttg * 4 + tl
                bank = 2 + (jb * 4 + tl) % 2
                for ct in range(16):
                    em.op("te", lambda t, ct=ct, tt=tt, bank=bank, ws=ws: t.matmul(
                        ps[bank][:, :], xnf_v[:, ct, tt * 128:(tt + 1) * 128],
                        wblk_v(ws)[:, ct, 0:512], start=(ct == 0), stop=(ct == 15)),
                        reads=[("wblk", ws), ("xnf",)], writes=[("psb", bank)])
                em.op("ve", lambda v, tl=tl, jb=jb, bank=bank: v.tensor_copy(
                    asm[tl][:, jb * 512:(jb + 1) * 512], ps[bank][:, :]),
                    reads=[("psb", bank)], writes=[("asm", tl, jb)])
        for tl in range(4):
            tt = ttg * 4 + tl
            em.dma("sy", lambda e, tt=tt, tl=tl: e.dma_start(
                out=vtt_s[tt * 128:(tt + 1) * 128, :], in_=asm[tl][:]),
                "vst", reads=[("asm", tl, jbx) for jbx in range(4)], writes=[("vtS",)])
        em.close_batch("vst")


    if "C" in phases and "E" in phases:
        for i in range(max(NCH, NT // 4)):
            if i < NCH:
                emit_C_tc(i)
            if i < NT // 4:
                emit_E_ttg(i)
        em.op("ve", lambda v: v.tensor_copy(DsumB[:], ltcc[:]), reads=[("ltc",)], writes=[("Dsum",)])
    elif "C" in phases:
        for i in range(NCH):
            emit_C_tc(i)
        em.op("ve", lambda v: v.tensor_copy(DsumB[:], ltcc[:]), reads=[("ltc",)], writes=[("Dsum",)])
    elif "E" in phases:
        for i in range(NT // 4):
            emit_E_ttg(i)

    if "D" in phases:
        for tc in range(NCH):
            for fb in range(4):
                ws = fb % 2
                em.dma("pe", lambda e, fb=fb, ws=ws, tc=tc: e.dma_start(
                    out=wblk_v(ws), in_=Wk[:, fb * 512:(fb + 1) * 512].rearrange("(ct p) f -> p ct f", p=128)),
                    "wgt", writes=[("wblk", ws)])
                em.close_batch("wgt")
                for fi in range(4):
                    ft = fb * 4 + fi
                    bank = ft % 2
                    es = 0
                    em.dma("sy", lambda e, ft=ft, tc=tc, es=es: e.dma_start(
                        out=ekt[es][:], in_=ek_s[ft * 128:(ft + 1) * 128, tc * CH:(tc + 1) * CH]),
                        "ekld", reads=[("ekS",)], writes=[("ekt", es)])
                    em.close_batch("ekld")
                    for ct in range(16):
                        em.op("te", lambda t, ct=ct, fi=fi, tc=tc, bank=bank, ws=ws: t.matmul(
                            ps[bank][:, :], wblk_v(ws)[:, ct, fi * 128:(fi + 1) * 128],
                            xnf_v[:, ct, tc * CH:(tc + 1) * CH], start=(ct == 0), stop=(ct == 15)),
                            reads=[("wblk", ws), ("xnf",)], writes=[("psb", bank)])
                    em.op("ve", lambda v, bank=bank, es=es: v.tensor_tensor(
                        kst[bank][:], ps[bank][:, :], ekt[es][:], ALU.mult),
                        reads=[("psb", bank), ("ekt", es)], writes=[("kst", bank)])
                    em.dma("sy", lambda e, ft=ft, tc=tc, bank=bank: e.dma_start(
                        out=ktf_s[ft * 128:(ft + 1) * 128, tc * CH:(tc + 1) * CH], in_=kst[bank][:]),
                        "kfst", reads=[("kst", bank)], writes=[("kfS",)])
                    em.close_batch("kfst")
                    for g in range(4):
                        pb = g % 2
                        em.op("te", lambda t, g=g, pb=pb, bank=bank: t.transpose(
                            pt[pb][:, 0:128], kst[bank][:, g * 128:(g + 1) * 128], idbf[:]),
                            reads=[("kst", bank), ("idbf",)], writes=[("pt", pb)])
                        em.op("se", lambda s_, g=g, pb=pb, ft=ft: s_.activation(
                            asm[g][:, ft * 128:(ft + 1) * 128], pt[pb][:, 0:128], AF.Identity),
                            reads=[("pt", pb)], writes=[("asm", g, ft)])
            for g in range(4):
                em.dma("sy", lambda e, tc=tc, g=g: e.dma_start(
                    out=ktt_s[tc * CH + g * 128:tc * CH + (g + 1) * 128, :], in_=asm[g][:]),
                    "ktst", reads=[("asm", g, ftx) for ftx in range(16)], writes=[("ktS",)])
            em.close_batch("ktst")

    if "F" in phases:
        for fb in range(4):
            ws = fb % 2
            em.dma("pe", lambda e, fb=fb, ws=ws: e.dma_start(
                out=wblk_v(ws), in_=Wr[:, fb * 512:(fb + 1) * 512].rearrange("(ct p) f -> p ct f", p=128)),
                "wgt", writes=[("wblk", ws)])
            em.close_batch("wgt")
            for fi in range(4):
                ft = fb * 4 + fi
                for tc in range(NCH):
                    bank = (ft * NCH + tc) % 2
                    es = 0
                    em.dma("sy", lambda e, ft=ft, tc=tc, es=es: e.dma_start(
                        out=ert[es][:], in_=er_s[ft * 128:(ft + 1) * 128, tc * CH:(tc + 1) * CH]),
                        "ekld", reads=[("ekS",)], writes=[("ert", es)])
                    em.close_batch("ekld")
                    for ct in range(16):
                        em.op("te", lambda t, ct=ct, fi=fi, tc=tc, bank=bank, ws=ws: t.matmul(
                            ps[bank][:, :], wblk_v(ws)[:, ct, fi * 128:(fi + 1) * 128],
                            xnf_v[:, ct, tc * CH:(tc + 1) * CH], start=(ct == 0), stop=(ct == 15)),
                            reads=[("wblk", ws), ("xnf",)], writes=[("psb", bank)])
                    em.op("se", lambda s_, bank=bank: s_.activation(spt[0][:], ps[bank][:, :], AF.Sigmoid),
                          reads=[("psb", bank)], writes=[("spt", 0)])
                    em.op("ve", lambda v, bank=bank, es=es: v.tensor_tensor(
                        kst[bank][:], spt[0][:], ert[es][:], ALU.mult),
                        reads=[("spt", 0), ("ert", es)], writes=[("kst", bank)])
                    em.dma("sy", lambda e, ft=ft, tc=tc, bank=bank: e.dma_start(
                        out=rtf_s[ft * 128:(ft + 1) * 128, tc * CH:(tc + 1) * CH], in_=kst[bank][:]),
                        "rfst", reads=[("kst", bank)], writes=[("rfS",)])
                    em.close_batch("rfst")
                    c0 = ft * NM + tc * MPC
                    em.op("ve", lambda v, c0=c0: v.tensor_copy(
                        eLx[:].rearrange("p (m i) -> p m i", m=MPC),
                        eLs[:, c0:c0 + MPC, None].broadcast_to([128, MPC, L])),
                        reads=[("eL", c0)], writes=[("eLx",)])
                    em.op("ve", lambda v, bank=bank: v.tensor_tensor(
                        kst[1 - bank][:], kst[bank][:], eLx[:], ALU.mult),
                        reads=[("kst", bank), ("eLx",)], writes=[("kst", 1 - bank)])
                    em.dma("sy", lambda e, ft=ft, tc=tc, bank=bank: e.dma_start(
                        out=rrf_s[ft * 128:(ft + 1) * 128, tc * CH:(tc + 1) * CH], in_=kst[1 - bank][:]),
                        "rrst", reads=[("kst", 1 - bank)], writes=[("rrS",)])
                    em.close_batch("rrst")

    em.barrier()
    poolA.close()

    poolB = ExitStack()
    sbB = lambda name, shape, dt=F32: poolB.enter_context(nc.sbuf_tensor(name, shape, dt))
    kf_c = [sbB(f"kf_c{i}", [128, 16 * CH], BF16) for i in range(2)]
    rf_c = [sbB(f"rf_c{i}", [128, 16 * CH], BF16) for i in range(2)]
    kt_c = [sbB(f"kt_c{i}", [128, (CH // 128) * D], BF16) for i in range(2)]
    vt_c = [sbB(f"vt_c{i}", [128, (CH // 128) * D], BF16) for i in range(2)]
    kstg = [sbB("kstg0", [32, D], BF16)]
    vstg = [sbB("vstg0", [32, D], BF16)]
    scm = sbB("scm", [128, 2048], BF16)
    SdecB = sbB("SdecB", [128, NPAIR * HS])
    SdecBF = sbB("SdecBF", [128, NPAIR * HS], BF16)
    ysb = [sbB(f"ysb{i}", [128, 16 * CH], BF16) for i in range(2)]
    ccst = sbB("ccst", [128, CCW])
    drm = sbB("drm", [128, 16])
    SgB = SdecB

    if "G" in phases:
        em.op("ve", lambda v: v.memset(SpackB[:], 0.0), writes=[("Spack",)])

        def g_loads(tc):
            sl = tc % 2
            em.dma("sy", lambda e, tc=tc, sl=sl: e.dma_start(
                out=kf_c[sl][:].rearrange("p (ct t) -> p ct t", ct=16),
                in_=ktf_s[:, tc * CH:(tc + 1) * CH].rearrange("(ct p) t -> p ct t", p=128)),
                "gld", reads=[("kfS",)], writes=[("kf_c", sl)])
            em.dma("sy", lambda e, tc=tc, sl=sl: e.dma_start(
                out=rf_c[sl][:].rearrange("p (ct t) -> p ct t", ct=16),
                in_=rtf_s[:, tc * CH:(tc + 1) * CH].rearrange("(ct p) t -> p ct t", p=128)),
                "gld", reads=[("rfS",)], writes=[("rf_c", sl)])
            em.dma("sy", lambda e, tc=tc, sl=sl: e.dma_start(
                out=kt_c[sl][:].rearrange("p (g f) -> p g f", g=CH // 128),
                in_=ktt_s[tc * CH:(tc + 1) * CH, :].rearrange("(g p) f -> p g f", p=128)),
                "gld", reads=[("ktS",)], writes=[("kt_c", sl)])
            em.dma("sy", lambda e, tc=tc, sl=sl: e.dma_start(
                out=vt_c[sl][:].rearrange("p (g f) -> p g f", g=CH // 128),
                in_=vtt_s[tc * CH:(tc + 1) * CH, :].rearrange("(g p) f -> p g f", p=128)),
                "gld", reads=[("vtS",)], writes=[("vt_c", sl)])
            em.close_batch("gld")

        g_loads(0)
        for tc in range(NCH):
            sl = tc % 2
            if tc + 1 < NCH:
                g_loads(tc + 1)
            kfv = kf_c[sl][:].rearrange("p (ct t) -> p ct t", ct=16)
            rfv = rf_c[sl][:].rearrange("p (ct t) -> p ct t", ct=16)
            ktv = kt_c[sl][:].rearrange("p (g f) -> p g f", g=CH // 128)
            vtv = vt_c[sl][:].rearrange("p (g f) -> p g f", g=CH // 128)

            def em_stage(g, sl=sl):
                em.dma("pe", lambda e, g=g, sl=sl: e.dma_start(
                    out=kstg[0][0:32, :], in_=kt_c[sl][96:128, g * D:(g + 1) * D]),
                    "stg", reads=[("kt_c", sl)], writes=[("kstg", 0)])
                em.dma("pe", lambda e, g=g, sl=sl: e.dma_start(
                    out=vstg[0][0:32, :], in_=vt_c[sl][96:128, g * D:(g + 1) * D]),
                    "stg", reads=[("vt_c", sl)], writes=[("vstg", 0)])
                em.close_batch("stg")

            def em_sdec(m, sl=sl, tc=tc):
                mg = tc * MPC + m
                em.op("ve", lambda v, mg=mg: v.tensor_copy(
                    ccst[:, 0:1024].rearrange("p (pr j) -> p pr j", pr=16),
                    eTC[:, mg::NM][:, :, None].broadcast_to([128, 16, HS])),
                    reads=[("eTC", prx * NM + tc * MPC) for prx in range(NPAIR)],
                    writes=[("ccst",)])
                em.op("ve", lambda v: v.tensor_mul(SdecB[:], SpackB[:], ccst[:, 0:1024]),
                      reads=[("Spack",), ("ccst",)], writes=[("SdecA",)])
                em.op("se", lambda s_: s_.activation(SdecBF[:], SdecB[:], AF.Identity),
                      reads=[("SdecA",)], writes=[("SdecBF",)])

            def em_scores(m, sl=sl, kfv=kfv, rfv=rfv):
                b = 32 * (m % 4) if (m % 4) != 3 else 0
                for h in range(HEADS):
                    hb = 64 * (h % 2)
                    prh = h // 2
                    bank = ps[h % 2]
                    col = (h // 2) * 32
                    em.op("te", lambda t, hb=hb, prh=prh, bank=bank, col=col, m=m, b=b: t.matmul(
                        bank[b:b + 32, col:col + 32],
                        kfv[hb:hb + 64, prh, m * L:(m + 1) * L],
                        rfv[hb:hb + 64, prh, m * L:(m + 1) * L],
                        start=True, stop=True),
                        reads=[("kf_c", sl), ("rf_c", sl)], writes=[("psb", h % 2)])

            def em_masks(m):
                b = 32 * (m % 4) if (m % 4) != 3 else 0
                for half in range(2):
                    em.op("ve", lambda v, half=half, b=b, m=m: v.tensor_tensor(
                        scm[b:b + 32, (m % 2) * 1024 + half * 512:(m % 2) * 1024 + (half + 1) * 512],
                        ps[half][b:b + 32, 0:512], mrepb[b:b + 32, :], ALU.mult),
                        reads=[("psb", half), ("mrepb",)], writes=[("scm", m % 2, half)])

            def em_y(m, sl=sl, vtv=vtv, rfv=rfv):
                g = m // 4
                b = 32 * (m % 4) if (m % 4) != 3 else 0
                for h in range(HEADS):
                    hb = 64 * (h % 2)
                    prh = h // 2
                    half = h % 2
                    scol = (m % 2) * 1024 + half * 512 + (h // 2) * 32
                    if m % 4 == 3:
                        vsrc = vstg[0][0:32, h * HS:(h + 1) * HS]
                        vread = ("vstg", 0)
                    else:
                        vsrc = vtv[b:b + 32, g, h * HS:(h + 1) * HS]
                        vread = ("vt_c", sl)
                    em.op("te", lambda t, vsrc=vsrc, scol=scol, hb=hb, prh=prh, b=b: t.matmul(
                        ps[2][hb:hb + 64, prh * 32:(prh + 1) * 32],
                        vsrc, scm[b:b + 32, scol:scol + 32],
                        start=True, stop=True),
                        reads=[vread, ("scm", m % 2, half)], writes=[("psb", 2)])
                    em.op("te", lambda t, hb=hb, prh=prh, m=m: t.matmul(
                        ps[5][hb:hb + 64, prh * 32:(prh + 1) * 32],
                        SdecBF[hb:hb + 64, prh * HS:(prh + 1) * HS],
                        rfv[hb:hb + 64, prh, m * L:(m + 1) * L],
                        start=True, stop=True),
                        reads=[("SdecBF",), ("rf_c", sl)], writes=[("psb", 5)])

            def em_sadd(m, sl=sl, ktv=ktv, vtv=vtv):
                g = m // 4
                b = 32 * (m % 4) if (m % 4) != 3 else 0
                for h in range(HEADS):
                    hb = 64 * (h % 2)
                    prh = h // 2
                    bank = 3 if prh < 8 else 4
                    col = (prh % 8) * HS
                    if m % 4 == 3:
                        ksrc = kstg[0][0:32, h * HS:(h + 1) * HS]
                        vsrc = vstg[0][0:32, h * HS:(h + 1) * HS]
                        kread, vread = ("kstg", 0), ("vstg", 0)
                    else:
                        ksrc = ktv[b:b + 32, g, h * HS:(h + 1) * HS]
                        vsrc = vtv[b:b + 32, g, h * HS:(h + 1) * HS]
                        kread, vread = ("kt_c", sl), ("vt_c", sl)
                    em.op("te", lambda t, ksrc=ksrc, vsrc=vsrc, hb=hb, col=col, bank=bank: t.matmul(
                        ps[bank][hb:hb + 64, col:col + HS], ksrc, vsrc,
                        start=True, stop=True),
                        reads=[kread, vread], writes=[("psb", bank)])

            def em_supd(m):
                em.op("ve", lambda v: v.tensor_add(SpackB[:, 0:512], SdecB[:, 0:512], ps[3][:, :]),
                      reads=[("SdecA",), ("psb", 3)], writes=[("Spack",)])
                em.op("ve", lambda v: v.tensor_add(SpackB[:, 512:1024], SdecB[:, 512:1024], ps[4][:, :]),
                      reads=[("SdecA",), ("psb", 4)], writes=[("Spack",)])

            def em_yevac(m, sl=sl):
                em.op("se", lambda s_, m=m: s_.activation(
                    ysb[sl][:].rearrange("p (pr t) -> p pr t", pr=16)[:, :, m * 32:(m + 1) * 32],
                    ps[2][:, :].rearrange("p (pr i) -> p pr i", pr=16), AF.Identity),
                    reads=[("psb", 2)], writes=[("ysb", sl, m)])
                em.op("ve", lambda v, m=m: v.tensor_tensor(
                    ysb[sl][:].rearrange("p (pr t) -> p pr t", pr=16)[:, :, m * 32:(m + 1) * 32],
                    ysb[sl][:].rearrange("p (pr t) -> p pr t", pr=16)[:, :, m * 32:(m + 1) * 32],
                    ps[5][:, :].rearrange("p (pr i) -> p pr i", pr=16), ALU.add),
                    reads=[("psb", 5), ("ysb", sl, m)], writes=[("ysb", sl, m)])

            em_stage(0)
            em_sdec(0)
            em_scores(0)
            em_masks(0)
            for m in range(MPC):
                if m % 4 == 0 and m > 0:
                    em_stage(m // 4)
                em_y(m)
                if m + 1 < MPC:
                    em_scores(m + 1)
                em_sadd(m)
                em_supd(m)
                em_yevac(m)
                if m + 1 < MPC:
                    em_masks(m + 1)
                    em_sdec(m + 1)
            em.dma("sy", lambda e, tc=tc, sl=sl: e.dma_start(
                out=y_s[:, tc * CH:(tc + 1) * CH].rearrange("(pr p) t -> p pr t", p=128),
                in_=ysb[sl][:].rearrange("p (pr t) -> p pr t", pr=16)),
                "yst", reads=[("ysb", sl, mx) for mx in range(MPC)], writes=[("yS",)])
            em.close_batch("yst")

    if "H" in phases:
        em.op("ve", lambda v: v.tensor_copy(ccst[:, 0:NPAIR * HS], SpackB[:]),
              reads=[("Spack",)], writes=[("ccst",)])
        em.op("ve", lambda v: v.tensor_copy(ccst[:, NPAIR * HS:CCW], DsumB[:]),
              reads=[("Dsum",)], writes=[("ccst",)])
        em.dma("pe", lambda e: e.dma_start(out=cc_in[:, :], in_=ccst[:]),
               "ccw", reads=[("ccst",)], writes=[("ccin",)])
        em.close_batch("ccw")
        em.op("pe", lambda g: g.collective_compute(
            "AllGather", ALU.bypass, replica_groups=[list(range(NCORE))],
            ins=[cc_in.ap().opt()], outs=[cc_out.ap().opt()]),
            reads=[("ccin",)], writes=[("ccout",)])
        em.op("ve", lambda v: v.memset(SinB[:], 0.0), writes=[("Sin",)])
        for r in range(NCORE - 1):
            em.dma("sy", lambda e, r=r: e.dma_start(
                out=ccst[:], in_=cc_out.ap()[r * 128:(r + 1) * 128, :]),
                "ccr", reads=[("ccout",)], writes=[("ccst",)])
            em.close_batch("ccr")
            em.op("ve", lambda v, r=r: v.tensor_scalar_mul(
                drm[:], ccst[:, NPAIR * HS:CCW], gmask[:, r:r + 1]),
                reads=[("ccst",), ("gmask",)], writes=[("drm",)])
            em.op("se", lambda s_: s_.activation(drm[:], drm[:], AF.Exp, scale=-1.0),
                  reads=[("drm",)], writes=[("drm",)])
            em.op("ve", lambda v, r=r: v.tensor_scalar_mul(
                SgB[:], ccst[:, 0:NPAIR * HS], gmask[:, r:r + 1]),
                reads=[("ccst",), ("gmask",)], writes=[("SdecA",)])
            for pr in range(NPAIR):
                em.op("ve", lambda v, pr=pr: v.scalar_tensor_tensor(
                    SinB[:, pr * HS:(pr + 1) * HS], SinB[:, pr * HS:(pr + 1) * HS],
                    drm[:, pr:pr + 1], SgB[:, pr * HS:(pr + 1) * HS], ALU.mult, ALU.add),
                    reads=[("Sin",), ("drm",), ("SdecA",)], writes=[("Sin",)])
        em.op("ve", lambda v: v.tensor_copy(SinBF[:], SinB[:]),
              reads=[("Sin",)], writes=[("SinBF",)])

    em.barrier()
    poolB.close()

    poolC = ExitStack()
    sbC = lambda name, shape, dt=F32: poolC.enter_context(nc.sbuf_tensor(name, shape, dt))
    yfx = [sbC(f"yfx{i}", [128, 16 * CH], BF16) for i in range(2)]
    rrc = [sbC(f"rrc{i}", [128, 16 * CH], BF16) for i in range(2)]
    ybuf = [sbC(f"ybuf{i}", [128, 16 * CH], BF16) for i in range(2)]
    wblkC = [sbC(f"wblkC{i}", [128, 16 * 512], BF16) for i in range(2)]
    xr = [sbC(f"xr{i}", [128, 512]) for i in range(2)]
    h1p = [sbC(f"h1p{i}", [128, 512]) for i in range(2)]
    h1row = [sbC(f"h1row{i}", [128, D]) for i in range(2)]
    lnsqC = sbC("lnsqC", [128, D])
    hnb = sbC("hnb", [128, D], BF16)
    hfstg = [sbC(f"hfstg{i}", [128, 16 * 128], BF16) for i in range(2)]
    wblkC_v = lambda i: wblkC[i][:].rearrange("p (ct f) -> p ct f", ct=16)

    if "I" in phases:
        def ln_tileC(src, srckey, dst, dstkey):
            em.op("se", lambda s_: s_.activation(lnsqC[:], src[:, 0:D], AF.Square,
                                                 accum_out=st8[:, 1:2]),
                  reads=[srckey], writes=[("lnsqC",), ("st8", 1)])
            em.op("ve", lambda v: v.reduce_sum(st8[:, 0:1], src[:, 0:D], axis=AX.X),
                  reads=[srckey], writes=[("st8", 0)])
            em.op("ve", lambda v: v.tensor_scalar_mul(st8[:, 0:1], st8[:, 0:1], 1.0 / D),
                  reads=[("st8", 0)], writes=[("st8", 0)])
            em.op("ve", lambda v: v.tensor_scalar(st8[:, 2:3], st8[:, 0:1], st8[:, 0:1],
                                                  -1.0, ALU.mult, ALU.mult),
                  reads=[("st8", 0)], writes=[("st8", 2)])
            em.op("ve", lambda v: v.scalar_tensor_tensor(st8[:, 1:2], st8[:, 1:2], 1.0 / D,
                                                         st8[:, 2:3], ALU.mult, ALU.add),
                  reads=[("st8", 1), ("st8", 2)], writes=[("st8", 1)])
            em.op("ve", lambda v: v.tensor_scalar_add(st8[:, 1:2], st8[:, 1:2], float(EPS)),
                  reads=[("st8", 1)], writes=[("st8", 1)])
            em.op("se", lambda s_: s_.activation(st8[:, 1:2], st8[:, 1:2], AF.Ln),
                  reads=[("st8", 1)], writes=[("st8", 1)])
            em.op("se", lambda s_: s_.activation(st8[:, 1:2], st8[:, 1:2], AF.Exp, scale=-0.5),
                  reads=[("st8", 1)], writes=[("st8", 1)])
            em.op("ve", lambda v: v.tensor_scalar(st8[:, 3:4], st8[:, 0:1], st8[:, 1:2],
                                                  -1.0, ALU.mult, ALU.mult),
                  reads=[("st8", 0), ("st8", 1)], writes=[("st8", 3)])
            em.op("se", lambda s_: s_.activation(dst[:, 0:D], src[:, 0:D], AF.Identity,
                                                 bias=st8[:, 3:4], scale=st8[:, 1:2]),
                  reads=[srckey, ("st8", 1), ("st8", 3)], writes=[dstkey])

        for tc in range(NCH):
            sl = tc % 2
            em.dma("sy", lambda e, tc=tc, sl=sl: e.dma_start(
                out=ybuf[sl][:].rearrange("p (pr t) -> p pr t", pr=16),
                in_=y_s[:, tc * CH:(tc + 1) * CH].rearrange("(pr p) t -> p pr t", p=128)),
                "ijld", reads=[("yS",)], writes=[("ybuf", sl)])
            em.dma("sy", lambda e, tc=tc, sl=sl: e.dma_start(
                out=rrc[sl][:].rearrange("p (pr t) -> p pr t", pr=16),
                in_=rrf_s[:, tc * CH:(tc + 1) * CH].rearrange("(pr p) t -> p pr t", p=128)),
                "ijld", reads=[("rrS",)], writes=[("rrc", sl)])
            em.close_batch("ijld")
            for pr in range(NPAIR):
                bank = pr % 2
                em.op("te", lambda t, pr=pr, bank=bank, sl=sl: t.matmul(
                    ps[bank][0:64, :], SinBF[0:64, pr * HS:(pr + 1) * HS],
                    rrc[sl][:].rearrange("p (q t) -> p q t", q=16)[0:64, pr, :],
                    start=True, stop=True),
                    reads=[("SinBF",), ("rrc", sl)], writes=[("psb", bank)])
                em.op("te", lambda t, pr=pr, bank=bank, sl=sl: t.matmul(
                    ps[bank][64:128, :], SinBF[64:128, pr * HS:(pr + 1) * HS],
                    rrc[sl][:].rearrange("p (q t) -> p q t", q=16)[64:128, pr, :],
                    start=True, stop=True),
                    reads=[("SinBF",), ("rrc", sl)], writes=[("psb", bank)])
                em.op("ve", lambda v, pr=pr, bank=bank, sl=sl: v.tensor_tensor(
                    yfx[sl][:].rearrange("p (q t) -> p q t", q=16)[:, pr, :],
                    ybuf[sl][:].rearrange("p (q t) -> p q t", q=16)[:, pr, :],
                    ps[bank][:, :], ALU.add),
                    reads=[("ybuf", sl), ("psb", bank)], writes=[("yfx", sl, pr)])
            for fb in range(4):
                ws = fb % 2
                em.dma("pe", lambda e, fb=fb, ws=ws, tc=tc: e.dma_start(
                    out=wblkC_v(ws), in_=Wo[:, fb * 512:(fb + 1) * 512].rearrange("(ct p) f -> p ct f", p=128)),
                    "wgt", writes=[("wblkC", ws)])
                em.close_batch("wgt")
                for tl in range(4):
                    bank = 2 + (fb * 4 + tl) % 2
                    xs_ = (fb * 4 + tl) % 2
                    em.dma("sy", lambda e, tc=tc, tl=tl, fb=fb, xs_=xs_: e.dma_start(
                        out=xr[xs_][:], in_=x_in[tc * CH + tl * 128:tc * CH + (tl + 1) * 128,
                                                 fb * 512:(fb + 1) * 512]),
                        "xrld", writes=[("xr", xs_)])
                    em.close_batch("xrld")
                    for jt in range(16):
                        em.op("te", lambda t, jt=jt, tl=tl, bank=bank, ws=ws, sl=sl: t.matmul(
                            ps[bank][:, :], yfx[sl][:].rearrange("p (q t) -> p q t", q=16)[:, jt, tl * 128:(tl + 1) * 128],
                            wblkC_v(ws)[:, jt, 0:512], start=(jt == 0), stop=(jt == 15)),
                            reads=[("yfx", sl, jt), ("wblkC", ws)], writes=[("psb", bank)])
                    hp = (fb * 4 + tl) % 2
                    em.op("ve", lambda v, bank=bank, hp=hp, xs_=xs_: v.tensor_tensor(
                        h1p[hp][:], ps[bank][:, :], xr[xs_][:], ALU.add),
                        reads=[("psb", bank), ("xr", xs_)], writes=[("h1p", hp)])
                    em.dma("sy", lambda e, tc=tc, tl=tl, fb=fb, hp=hp: e.dma_start(
                        out=h1_s[tc * CH + tl * 128:tc * CH + (tl + 1) * 128,
                                 fb * 512:(fb + 1) * 512], in_=h1p[hp][:]),
                        "h1st", reads=[("h1p", hp)], writes=[("h1S",)])
                    em.close_batch("h1st")
            for tl in range(4):
                hs_ = tl % 2
                em.dma("sy", lambda e, tc=tc, tl=tl, hs_=hs_: e.dma_start(
                    out=h1row[hs_][:], in_=h1_s[tc * CH + tl * 128:tc * CH + (tl + 1) * 128, :]),
                    "h1rl", reads=[("h1S",)], writes=[("h1row", hs_)])
                em.close_batch("h1rl")
                ln_tileC(h1row[hs_], ("h1row", hs_), h1row[hs_], ("h1row", hs_))
                em.op("ve", lambda v, hs_=hs_: v.tensor_copy(hnb[:], h1row[hs_][:]),
                      reads=[("h1row", hs_)], writes=[("hnb",)])
                for ft in range(16):
                    pb = ft % 2
                    em.op("te", lambda t, ft=ft, pb=pb: t.transpose(
                        pt[pb][:, 0:128], hnb[:, ft * 128:(ft + 1) * 128], idbf[:]),
                        reads=[("hnb",), ("idbf",)], writes=[("pt", pb)])
                    fsl = tl % 2
                    em.op("se", lambda s_, ft=ft, pb=pb, fsl=fsl: s_.activation(
                        hfstg[fsl][:, ft * 128:(ft + 1) * 128], pt[pb][:, 0:128], AF.Identity),
                        reads=[("pt", pb)], writes=[("hfstg", fsl, ft)])
                em.dma("sy", lambda e, tc=tc, tl=tl: e.dma_start(
                    out=hf_s[:, tc * CH + tl * 128:tc * CH + (tl + 1) * 128].rearrange("(ft p) t -> p ft t", p=128),
                    in_=hfstg[tl % 2][:].rearrange("p (ft t) -> p ft t", ft=16)),
                    "hfst", reads=[("hfstg", tl % 2, ftx) for ftx in range(16)], writes=[("hfS",)])
                em.close_batch("hfst")

    em.barrier()
    poolC.close()

    poolD = ExitStack()
    sbD = lambda name, shape, dt=F32: poolD.enter_context(nc.sbuf_tensor(name, shape, dt))
    hfc = [sbD(f"hfc{i}", [128, 16 * CH], BF16) for i in range(2)]
    wgu = [sbD(f"wgu{i}", [128, 16 * (DI // 2)], BF16) for i in range(4)]
    wdd = [sbD(f"wdd{i}", [128, 6 * 512], BF16) for i in range(2)]
    hexp = sbD("hexp", [128, 6 * TB], BF16)
    accs = sbD("accs", [128, NTTB * D])
    sgs = sbD("sgs", [128, 2 * 512], BF16)
    rwT = sbD("rwT", [128, 16 * NE], BF16)
    rwstage = sbD("rwstage", [32, D], BF16)
    wde = sbD("wde", [128, NTTB * NE])
    rst = sbD("rst", [128, 16])
    lgt = sbD("lgt", [128, 2 * NE])
    h1tD = sbD("h1tD", [128, D])

    if "K" in phases:
        em.dma("pe", lambda e: e.dma_start(out=rwstage[0:32, :], in_=rweights),
               "rwl", writes=[("rwstage",)])
        em.close_batch("rwl")
        for ct in range(16):
            pb = ct % 2
            em.op("te", lambda t, ct=ct, pb=pb: t.transpose(
                pt[pb][:, 0:32], rwstage[0:32, ct * 128:(ct + 1) * 128], idbf[0:32, 0:32]),
                reads=[("rwstage",), ("idbf",)], writes=[("pt", pb)])
            em.op("se", lambda s_, ct=ct, pb=pb: s_.activation(
                rwT[:, ct * NE:(ct + 1) * NE], pt[pb][:, 0:32], AF.Identity),
                reads=[("pt", pb)], writes=[("rwT",)])

        GRPB = NCHB * 6
        DWNB = NTTB * 4
        for blk_i in range(NB):
            for tch in range(NCHB):
                sl = tch
                em.dma("sy", lambda e, blk_i=blk_i, tch=tch, sl=sl: e.dma_start(
                    out=hfc[sl][:].rearrange("p (ct t) -> p ct t", ct=16),
                    in_=hf_s[:, blk_i * TB + tch * CH:blk_i * TB + (tch + 1) * CH].rearrange(
                        "(ct p) t -> p ct t", p=128)),
                    "hfl", reads=[("hfS",)], writes=[("hfc", sl)])
                em.close_batch("hfl")
            em.op("ve", lambda v: v.memset(accs[:], 0.0), writes=[("accs",)])
            for tch in range(NCHB):
                for tl in range(4):
                    ttg = tch * 4 + tl
                    bank = 5
                    for ct in range(16):
                        em.op("te", lambda t, ct=ct, tch=tch, tl=tl, bank=bank: t.matmul(
                            ps[bank][:, 0:NE],
                            hfc[tch][:].rearrange("p (c t) -> p c t", c=16)[:, ct, tl * 128:(tl + 1) * 128],
                            rwT[:, ct * NE:(ct + 1) * NE], start=(ct == 0), stop=(ct == 15)),
                            reads=[("hfc", tch), ("rwT",)], writes=[("psb", bank)])
                    em.op("ve", lambda v, bank=bank: v.tensor_copy(lgt[:, 0:NE], ps[bank][:, 0:NE]),
                          reads=[("psb", bank)], writes=[("lgt", 0)])
                    em.op("ve", lambda v: v.reduce_max(rst[:, 0:1], lgt[:, 0:NE], axis=AX.X),
                          reads=[("lgt", 0)], writes=[("rst", 0)])
                    em.op("ve", lambda v: v.tensor_scalar_mul(rst[:, 1:2], rst[:, 0:1], -1.0),
                          reads=[("rst", 0)], writes=[("rst", 1)])
                    em.op("se", lambda s_: s_.activation(lgt[:, NE:2 * NE], lgt[:, 0:NE], AF.Exp,
                                                         bias=rst[:, 1:2]),
                          reads=[("lgt", 0), ("rst", 1)], writes=[("lgt", 1)])
                    em.op("ve", lambda v: v.max(rst[:, 8:16], lgt[:, NE:2 * NE]),
                          reads=[("lgt", 1)], writes=[("rst", 8)])
                    em.op("ve", lambda v: v.reduce_sum(rst[:, 2:3], rst[:, 8:16], axis=AX.X),
                          reads=[("rst", 8)], writes=[("rst", 2)])
                    em.op("ve", lambda v: v.reciprocal(rst[:, 3:4], rst[:, 2:3]),
                          reads=[("rst", 2)], writes=[("rst", 3)])
                    em.op("ve", lambda v: v.tensor_scalar(lgt[:, 0:NE], lgt[:, NE:2 * NE],
                                                          rst[:, 15:16], None, ALU.is_ge),
                          reads=[("lgt", 1), ("rst", 8)], writes=[("lgt", 0)])
                    em.op("ve", lambda v: v.tensor_mul(lgt[:, 0:NE], lgt[:, 0:NE], lgt[:, NE:2 * NE]),
                          reads=[("lgt", 0), ("lgt", 1)], writes=[("lgt", 0)])
                    em.op("ve", lambda v, ttg=ttg: v.tensor_scalar_mul(
                        wde[:, ttg * NE:(ttg + 1) * NE], lgt[:, 0:NE], rst[:, 3:4]),
                        reads=[("lgt", 0), ("rst", 3)], writes=[("wde", ttg)])
            for e_i in range(NE):
                for half in range(2):
                    em.dma("pe", lambda e, e_i=e_i, half=half: e.dma_start(
                        out=wgu[half][:].rearrange("p (ct i) -> p ct i", ct=16),
                        in_=wgate[e_i][:, half * 384:(half + 1) * 384].rearrange("(ct p) i -> p ct i", p=128)),
                        "ffg", writes=[("wgu", half)])
                    em.close_batch("ffg")
                    em.dma("pe", lambda e, e_i=e_i, half=half: e.dma_start(
                        out=wgu[2 + half][:].rearrange("p (ct i) -> p ct i", ct=16),
                        in_=wup[e_i][:, half * 384:(half + 1) * 384].rearrange("(ct p) i -> p ct i", p=128)),
                        "ffu", writes=[("wgu", 2 + half)])
                    em.close_batch("ffu")
                for gno in range(GRPB):
                    tch, it = divmod(gno, 6)
                    pg = ps[gno % 2]
                    pu = ps[2 + gno % 2]
                    gh = it // 3
                    gi = it % 3
                    em.op("te", lambda t, tch=tch, gi=gi, gh=gh, pg=pg, gno=gno: [
                        t.matmul(pg[:, :],
                                 wgu[gh][:].rearrange("p (c i) -> p c i", c=16)[:, ct, gi * 128:(gi + 1) * 128],
                                 hfc[tch][:].rearrange("p (c t) -> p c t", c=16)[:, ct, :],
                                 start=(ct == 0), stop=(ct == 15)) for ct in range(16)][-1],
                        reads=[("wgu", gh), ("hfc", tch)], writes=[("psb", gno % 2)])
                    em.op("te", lambda t, tch=tch, gi=gi, gh=gh, pu=pu, gno=gno: [
                        t.matmul(pu[:, :],
                                 wgu[2 + gh][:].rearrange("p (c i) -> p c i", c=16)[:, ct, gi * 128:(gi + 1) * 128],
                                 hfc[tch][:].rearrange("p (c t) -> p c t", c=16)[:, ct, :],
                                 start=(ct == 0), stop=(ct == 15)) for ct in range(16)][-1],
                        reads=[("wgu", 2 + gh), ("hfc", tch)], writes=[("psb", 2 + gno % 2)])
                    em.op("se", lambda s_, gno=gno, pg=pg: s_.activation(
                        sgs[:, (gno % 2) * 512:(gno % 2) * 512 + 512], pg[:, :], AF.Silu),
                        reads=[("psb", gno % 2)], writes=[("sgs", gno % 2)])
                    em.op("ve", lambda v, gno=gno, tch=tch, it=it, pu=pu: v.tensor_tensor(
                        hexp[:].rearrange("p (i t) -> p i t", i=6)[:, it, tch * CH:(tch + 1) * CH],
                        sgs[:, (gno % 2) * 512:(gno % 2) * 512 + 512], pu[:, :], ALU.mult),
                        reads=[("sgs", gno % 2), ("psb", 2 + gno % 2)], writes=[("hexp", gno)])
                for fo in range(4):
                    wsd = fo % 2
                    em.dma("pe", lambda e, e_i=e_i, fo=fo, wsd=wsd: e.dma_start(
                        out=wdd[wsd][:].rearrange("p (i f) -> p i f", i=6),
                        in_=wdown[e_i][:, fo * 512:(fo + 1) * 512].rearrange("(i p) f -> p i f", p=128)),
                        "ffd", writes=[("wdd", wsd)])
                    em.close_batch("ffd")
                    for tt in range(NTTB):
                        pd = 4 + (fo * NTTB + tt) % 2
                        em.op("te", lambda t, tt=tt, fo=fo, wsd=wsd, pd=pd: [
                            t.matmul(ps[pd][:, :],
                                     hexp[:].rearrange("p (i t) -> p i t", i=6)[:, it_, tt * 128:(tt + 1) * 128],
                                     wdd[wsd][:].rearrange("p (i f) -> p i f", i=6)[:, it_, :],
                                     start=(it_ == 0), stop=(it_ == 5)) for it_ in range(6)][-1],
                            reads=[("wdd", wsd)] + [("hexp", tc_ * 6 + i_) for tc_ in range(NCHB) for i_ in range(6)],
                            writes=[("psb", pd)])
                        em.op("ve", lambda v, tt=tt, fo=fo, e_i=e_i, pd=pd: v.scalar_tensor_tensor(
                            accs[:].rearrange("p (tt f) -> p tt f", tt=NTTB)[:, tt, fo * 512:(fo + 1) * 512],
                            ps[pd][:, :], wde[:, tt * NE + e_i:tt * NE + e_i + 1],
                            accs[:].rearrange("p (tt f) -> p tt f", tt=NTTB)[:, tt, fo * 512:(fo + 1) * 512],
                            ALU.mult, ALU.add),
                            reads=[("psb", pd), ("wde", tt)], writes=[("accs",)])
            for tt in range(NTTB):
                trow = blk_i * TB + tt * 128
                em.dma("sy", lambda e, trow=trow: e.dma_start(
                    out=h1tD[:], in_=h1_s[trow:trow + 128, :]), "h1ld", reads=[("h1S",)], writes=[("h1tD",)])
                em.close_batch("h1ld")
                em.op("ve", lambda v, tt=tt: v.tensor_tensor(
                    h1tD[:], accs[:].rearrange("p (q f) -> p q f", q=NTTB)[:, tt, :],
                    h1tD[:], ALU.add),
                    reads=[("accs",), ("h1tD",)], writes=[("h1tD",)])
                em.dma("sy", lambda e, trow=trow: e.dma_start(
                    out=out_o[trow:trow + 128, :], in_=h1tD[:]), "outs", reads=[("h1tD",)])
                em.close_batch("outs")

    em.barrier()
    poolD.close()

    taps_f32 = {"ek": ek_s, "er": er_s, "h1": h1_s}
    taps_bf = {"q": q_s, "ktf": ktf_s, "ktt": ktt_s, "vtt": vtt_s,
               "rtf": rtf_s, "rrf": rrf_s, "y": y_s, "hf": hf_s}
    taps_keys = {"ek": ("ekS",), "er": ("ekS",), "h1": ("h1S",), "q": ("qS",),
                 "ktf": ("kfS",), "ktt": ("ktS",), "vtt": ("vtS",), "rtf": ("rfS",),
                 "rrf": ("rrS",), "y": ("yS",), "hf": ("hfS",)}
    for name in dbg:
        if name in taps_f32:
            em.dma("sy", lambda e, name=name: e.dma_start(out=dbg[name], in_=taps_f32[name]),
                   "dbgs", reads=[taps_keys[name]])
            em.close_batch("dbgs")
        elif name in taps_bf:
            em.dma("pe", lambda e, name=name: e.dma_start(out=dbg[name], in_=taps_bf[name]),
                   "dbgp", reads=[taps_keys[name]])
            em.close_batch("dbgp")
        elif name == "eTC":
            em.dma("sy", lambda e: e.dma_start(out=dbg["eTC"], in_=eTC[:]), "dbgs",
                   reads=[("eTC", c0x) for c0x in range(0, 16 * NM, MPC)])
            em.close_batch("dbgs")
        elif name == "eL":
            em.dma("sy", lambda e: e.dma_start(out=dbg["eL"], in_=eLs[:]), "dbgs",
                   reads=[("eL", c0x) for c0x in range(0, 16 * NM, MPC)])
            em.close_batch("dbgs")
        elif name == "Dsum":
            em.dma("sy", lambda e: e.dma_start(out=dbg["Dsum"], in_=DsumB[:]), "dbgs",
                   reads=[("Dsum",)])
            em.close_batch("dbgs")
        elif name == "Spack":
            em.dma("sy", lambda e: e.dma_start(out=dbg["Spack"], in_=SpackB[:]), "dbgs",
                   reads=[("Spack",)])
            em.close_batch("dbgs")
        elif name == "Sin":
            em.dma("sy", lambda e: e.dma_start(out=dbg["Sin"], in_=SinB[:]), "dbgs",
                   reads=[("Sin",)])
            em.close_batch("dbgs")
    em.close_batch("dbgs")

    fs = []
    if "K" in phases:
        fs.append("outs")
    if dbg:
        fs += ["dbgs", "dbgp"]
    em.emit(nc, stack, final_stream=fs)
    return nc, out_names, stack, em



def _host_consts(rank):
    ident = np.eye(128, dtype=np.float32)
    c = np.zeros((32, 32), np.float32)
    for a in range(32):
        c[a, a:] = 1.0
    mrepb = np.tile(c, (4, 16)).astype(np.float32)
    mseg = np.ones((128, CH), np.float32)
    mseg[:, 0::L] = 0.0
    gmask = np.zeros((128, NCORE), np.float32)
    gmask[:, :rank] = 1.0
    return ident, mrepb, mseg, gmask


def kernel(L0_attn_x, L0_attn_Wx, L0_attn_Ww, L0_attn_bw, L0_attn_Wk, L0_attn_Wv,
           L0_attn_Wr, L0_attn_Wo, L0_attn_gamma, L0_attn_beta, L0_ffn_router_w,
           L0_ffn_w_gate, L0_ffn_w_up, L0_ffn_w_down):
    T = L0_attn_x.shape[1] // NCORE
    x = np.ascontiguousarray(np.asarray(L0_attn_x, dtype=np.float32).reshape(NCORE * T, D))
    f32 = lambda a: np.ascontiguousarray(np.asarray(a, dtype=np.float32))
    base = dict(
        Wx=f32(L0_attn_Wx), Ww=f32(L0_attn_Ww),
        bw=np.ascontiguousarray(f32(L0_attn_bw).reshape(16, 128).T),
        Wk=f32(L0_attn_Wk), Wv=f32(L0_attn_Wv), Wr=f32(L0_attn_Wr), Wo=f32(L0_attn_Wo),
        gamma=np.ascontiguousarray(f32(L0_attn_gamma).reshape(16, 128).T),
        beta=np.ascontiguousarray(f32(L0_attn_beta).reshape(16, 128).T),
        rw=f32(L0_ffn_router_w), wg=f32(L0_ffn_w_gate), wu=f32(L0_ffn_w_up),
        wdn=f32(L0_ffn_w_down),
    )
    nc, out_names, stack, em = build_program(T=T)
    in_maps = []
    for r in range(NCORE):
        ident, mrepb, mseg, gmask = _host_consts(r)
        m = dict(base)
        m["x"] = x[r * T:(r + 1) * T]
        m["identity"] = ident
        m["mrepb_c"] = mrepb
        m["mseg_c"] = mseg
        m["gmask_c"] = gmask
        in_maps.append(m)
    res = run_bass_kernel_spmd(nc, in_maps, core_ids=list(range(NCORE)))
    stack.close()
    out = np.concatenate([np.asarray(res.results[r]["out"]) for r in range(NCORE)], axis=0)
    return out.reshape(1, NCORE * T, D).astype(np.float32)


# version 07 (final)
# speedup vs baseline: 1.0091x; 1.0091x over previous
import numpy as np
import ml_dtypes
from contextlib import ExitStack

import concourse.bass as bass
import concourse.mybir as mybir
from concourse.bass_utils import run_bass_kernel_spmd

F32 = mybir.dt.float32
BF16 = mybir.dt.bfloat16
AF = mybir.ActivationFunctionType
ALU = mybir.AluOpType
AX = mybir.AxisListType

D = 2048
NCORE = 8
HEADS = 32
HS = 64
L = 32
CH = 512
NE = 32
DI = 768
EPS = 1e-5
ENGINES = ("te", "ve", "se", "pe", "sy")
G_SKIP = ""


class Emitter:

    def __init__(self):
        self.plan = []
        self.last_w = {}
        self.readers = {}
        self.batch_open = {}
        self.batch_count = {}
        self.batch_sizes = {}
        self.cur_batch_size = {}
        self.barrier_deps = []

    def _record(self, rec, reads, writes, me):
        deps = list(self.barrier_deps)
        for r in reads:
            w = self.last_w.get(r)
            if w is not None:
                deps.append(w)
        for w in writes:
            lw = self.last_w.get(w)
            if lw is not None:
                deps.append(lw)
            for src in self.readers.get(w, {}).values():
                deps.append(src)
        rec["deps"] = deps
        self.plan.append(rec)
        for r in reads:
            self.readers.setdefault(r, {})[me[0:2] if me[0] == "dma" else ("e", me[1])] = me
        for w in writes:
            self.last_w[w] = me
            self.readers[w] = {}

    def op(self, eng, fn, reads=(), writes=(), tag=""):
        oid = len(self.plan)
        rec = dict(eng=eng, fn=fn, is_dma=False, stream=None, need_inc=False,
                   oid=oid, tag=tag)
        self._record(rec, reads, writes, ("eng", eng, oid))
        return oid

    def dma(self, eng, fn, stream, reads=(), writes=(), tag=""):
        oid = len(self.plan)
        extra = []
        if not self.batch_open.get(stream, False):
            self.batch_open[stream] = True
            self.cur_batch_size[stream] = 0
            nb = self.batch_count.get(stream, 0)
            if nb > 0:
                extra.append(("dma", stream, nb - 1))
        self.cur_batch_size[stream] += 1
        batch_idx = self.batch_count.get(stream, 0)
        rec = dict(eng=eng, fn=fn, is_dma=True, stream=stream, batch=batch_idx,
                   need_inc=False, oid=oid, tag=tag)
        save = self.barrier_deps
        self.barrier_deps = save + extra
        self._record(rec, reads, writes, ("dma", stream, batch_idx))
        self.barrier_deps = save
        rec["deps"] = [d for d in rec["deps"]
                       if not (d[0] == "dma" and d[1] == stream and d not in extra)]
        return oid

    def close_batch(self, stream):
        if self.batch_open.get(stream, False):
            self.batch_open[stream] = False
            self.batch_sizes.setdefault(stream, []).append(self.cur_batch_size[stream])
            self.batch_count[stream] = self.batch_count.get(stream, 0) + 1

    def barrier(self):
        for s in list(self.batch_open):
            self.close_batch(s)
        deps = []
        last_by_eng = {}
        for rec in self.plan:
            if not rec["is_dma"]:
                last_by_eng[rec["eng"]] = rec["oid"]
        for e, oid in last_by_eng.items():
            deps.append(("eng", e, oid))
        for s, nb in self.batch_count.items():
            if nb > 0:
                deps.append(("dma", s, nb - 1))
        self.barrier_deps = deps

    def emit(self, nc, stack, final_stream=()):
        for s in list(self.batch_open):
            self.close_batch(s)
        for rec in self.plan:
            for dep in rec["deps"]:
                if dep[0] == "eng":
                    _, peng, poid = dep
                    if peng == rec["eng"] and peng == "te":
                        continue
                    self.plan[poid]["need_inc"] = True
        inc_val = {}
        cnt = {e: 0 for e in ENGINES}
        for rec in self.plan:
            if not rec["is_dma"] and rec["need_inc"]:
                cnt[rec["eng"]] += 1
                inc_val[rec["oid"]] = cnt[rec["eng"]]
        cum = {}
        for s, sizes in self.batch_sizes.items():
            c, lst = 0, []
            for sz in sizes:
                c += sz
                lst.append(c)
            cum[s] = lst
        esem = {e: stack.enter_context(nc.semaphore(f"es_{e}")) for e in ENGINES}
        dsem = {s: stack.enter_context(nc.semaphore(f"ds_{s}")) for s in self.batch_sizes}

        def dep_to_wait(dep, myeng):
            if dep[0] == "eng":
                _, peng, poid = dep
                if peng == myeng and peng == "te":
                    return None
                return (("e", peng), inc_val[poid])
            _, s, b = dep
            return (("d", s), 16 * cum[s][b])

        per_eng = {e: [] for e in ENGINES}
        for rec in self.plan:
            per_eng[rec["eng"]].append(rec)
        nwaits = [0]

        def make_section(eng, recs):
            def section(engobj):
                wm = {}
                for rec in recs:
                    for dep in rec["deps"]:
                        w = dep_to_wait(dep, eng)
                        if w is None:
                            continue
                        key, val = w
                        if wm.get(key, 0) >= val:
                            continue
                        engobj.wait_ge(esem[key[1]] if key[0] == "e" else dsem[key[1]], val)
                        wm[key] = val
                        nwaits[0] += 1
                    inst = rec["fn"](engobj)
                    if rec["is_dma"]:
                        inst.then_inc(dsem[rec["stream"]], 16)
                    elif rec["need_inc"]:
                        inst.then_inc(esem[eng], 1)
                if eng == "sy" and final_stream:
                    for fs_ in final_stream:
                        if fs_ in cum:
                            engobj.wait_ge(dsem[fs_], 16 * cum[fs_][-1])
            return section

        blk = stack.enter_context(nc.Block())
        blk.tensor(make_section("te", per_eng["te"]))
        blk.vector(make_section("ve", per_eng["ve"]))
        blk.scalar(make_section("se", per_eng["se"]))
        blk.gpsimd(make_section("pe", per_eng["pe"]))
        blk.sync(make_section("sy", per_eng["sy"]))
        self.stats = dict(n_ops=len(self.plan), n_waits=nwaits[0],
                          n_incs=sum(cnt.values()))



def build_program(T=2048, phases="ABCDEFGHIJK", debug=()):
    nc = bass.Bass("TRN2", target_bir_lowering=False, debug=False)
    em = Emitter()
    NT = T // 128
    NCH = T // CH
    NM = T // L
    MPC = CH // L
    NPAIR = HEADS // 2
    TB = min(1024, T)
    NB = T // TB
    NTTB = TB // 128
    NCHB = TB // CH
    CCW = NPAIR * HS + 16

    ap = lambda name, shape, kind: nc.dram_tensor(name, shape, F32, kind=kind).ap()
    x_in = ap("x", [T, D], "ExternalInput")
    Wx = ap("Wx", [D, D], "ExternalInput")
    Ww = ap("Ww", [D, D], "ExternalInput")
    bw = ap("bw", [128, 16], "ExternalInput")
    Wk = ap("Wk", [D, D], "ExternalInput")
    Wv = ap("Wv", [D, D], "ExternalInput")
    Wr = ap("Wr", [D, D], "ExternalInput")
    Wo = ap("Wo", [D, D], "ExternalInput")
    gamma = ap("gamma", [128, 16], "ExternalInput")
    beta = ap("beta", [128, 16], "ExternalInput")
    rweights = ap("rw", [NE, D], "ExternalInput")
    wgate = ap("wg", [NE, D, DI], "ExternalInput")
    wup = ap("wu", [NE, D, DI], "ExternalInput")
    wdown = ap("wdn", [NE, DI, D], "ExternalInput")
    ident_c = ap("identity", [128, 128], "ExternalInput")
    mrepb_c = ap("mrepb_c", [128, 512], "ExternalInput")
    mseg_c = ap("mseg_c", [128, CH], "ExternalInput")
    gmask_c = ap("gmask_c", [128, NCORE], "ExternalInput")
    out_o = ap("out", [T, D], "ExternalOutput")

    out_names = ["out"]
    dbg = {}
    for name, shape in debug:
        dbg[name] = nc.dram_tensor("dbg_" + name, shape, F32, kind="ExternalOutput").ap()
        out_names.append("dbg_" + name)

    sc = lambda name, shape, dt: nc.dram_tensor(name, shape, dt).ap()
    q_s = sc("q_s", [D, T], BF16)
    ek_s = sc("ek_s", [D, T], F32)
    er_s = sc("er_s", [D, T], F32)
    ktf_s = sc("ktf_s", [D, T], BF16)
    ktt_s = sc("ktt_s", [T, D], BF16)
    vtt_s = sc("vtt_s", [T, D], BF16)
    rtf_s = sc("rtf_s", [D, T], BF16)
    rrf_s = sc("rrf_s", [D, T], BF16)
    y_s = sc("y_s", [D, T], BF16)
    h1_s = sc("h1_s", [T, D], F32)
    hf_s = sc("hf_s", [D, T], BF16)
    cc_in = nc.dram_tensor("cc_in", [128, CCW], F32)
    cc_out = nc.dram_tensor("cc_out", [NCORE * 128, CCW], F32)

    stack = ExitStack()
    sb = lambda name, shape, dt=F32: stack.enter_context(nc.sbuf_tensor(name, shape, dt))

    gamf = sb("gamf", [128, 16]); betf = sb("betf", [128, 16]); bwf = sb("bwf", [128, 16])
    idbf = sb("idbf", [128, 128], BF16)
    idf32 = sb("idf32", [128, 128])
    mrepb = sb("mrepb", [128, 512])
    mseg = sb("mseg", [128, CH])
    gmask = sb("gmask", [128, NCORE])
    eTC = sb("eTC", [128, 16 * NM]); eLs = sb("eLs", [128, 16 * NM])
    ltcc = sb("ltcc", [128, 16]); DsumB = sb("DsumB", [128, 16])
    SpackB = sb("SpackB", [128, NPAIR * HS]); SinB = sb("SinB", [128, NPAIR * HS])
    SinBF = sb("SinBF", [128, NPAIR * HS], BF16)
    st8 = sb("st8", [128, 16])

    ps = [stack.enter_context(nc.psum_tensor(f"ps{i}", [128, 512], F32)) for i in range(6)]
    pt = [stack.enter_context(nc.psum_tensor(f"pt{i}", [128, 1024], BF16)) for i in range(2)]

    em.dma("sy", lambda e: e.dma_start(out=gamf[:], in_=gamma), "cst", writes=[("gamf",)])
    em.dma("sy", lambda e: e.dma_start(out=betf[:], in_=beta), "cst", writes=[("betf",)])
    em.dma("sy", lambda e: e.dma_start(out=bwf[:], in_=bw), "cst", writes=[("bwf",)])
    em.dma("sy", lambda e: e.dma_start(out=idf32[:], in_=ident_c), "cst", writes=[("idf32",)])
    em.dma("sy", lambda e: e.dma_start(out=mrepb[:], in_=mrepb_c), "cst", writes=[("mrepb",)])
    em.dma("sy", lambda e: e.dma_start(out=mseg[:], in_=mseg_c), "cst", writes=[("mseg",)])
    em.dma("sy", lambda e: e.dma_start(out=gmask[:], in_=gmask_c), "cst", writes=[("gmask",)])
    em.close_batch("cst")
    em.op("ve", lambda v: v.tensor_copy(idbf[:], idf32[:]), reads=[("idf32",)], writes=[("idbf",)])

    def ln_tile(src, srckey, dst, dstkey, extra_reads=()):
        em.op("se", lambda s_: s_.activation(lnsq[:], src[:, 0:D], AF.Square,
                                             accum_out=st8[:, 1:2]),
              reads=[srckey] + list(extra_reads), writes=[("lnsq",), ("st8", 1)])
        em.op("ve", lambda v: v.reduce_sum(st8[:, 0:1], src[:, 0:D], axis=AX.X),
              reads=[srckey], writes=[("st8", 0)])
        em.op("ve", lambda v: v.tensor_scalar_mul(st8[:, 0:1], st8[:, 0:1], 1.0 / D),
              reads=[("st8", 0)], writes=[("st8", 0)])
        em.op("ve", lambda v: v.tensor_scalar(st8[:, 2:3], st8[:, 0:1], st8[:, 0:1],
                                              -1.0, ALU.mult, ALU.mult),
              reads=[("st8", 0)], writes=[("st8", 2)])
        em.op("ve", lambda v: v.scalar_tensor_tensor(st8[:, 1:2], st8[:, 1:2], 1.0 / D,
                                                     st8[:, 2:3], ALU.mult, ALU.add),
              reads=[("st8", 1), ("st8", 2)], writes=[("st8", 1)])
        em.op("ve", lambda v: v.tensor_scalar_add(st8[:, 1:2], st8[:, 1:2], float(EPS)),
              reads=[("st8", 1)], writes=[("st8", 1)])
        em.op("se", lambda s_: s_.activation(st8[:, 1:2], st8[:, 1:2], AF.Ln),
              reads=[("st8", 1)], writes=[("st8", 1)])
        em.op("se", lambda s_: s_.activation(st8[:, 1:2], st8[:, 1:2], AF.Exp, scale=-0.5),
              reads=[("st8", 1)], writes=[("st8", 1)])
        em.op("ve", lambda v: v.tensor_scalar(st8[:, 3:4], st8[:, 0:1], st8[:, 1:2],
                                              -1.0, ALU.mult, ALU.mult),
              reads=[("st8", 0), ("st8", 1)], writes=[("st8", 3)])
        em.op("se", lambda s_: s_.activation(dst[:, 0:D], src[:, 0:D], AF.Identity,
                                             bias=st8[:, 3:4], scale=st8[:, 1:2]),
              reads=[srckey, ("st8", 1), ("st8", 3)], writes=[dstkey])

    poolA = ExitStack()
    sbA = lambda name, shape, dt=F32: poolA.enter_context(nc.sbuf_tensor(name, shape, dt))
    xnf = sbA("xnf", [128, 16 * T], BF16)
    wblk = [sbA(f"wblk{i}", [128, 16 * 512], BF16) for i in range(2)]
    xt = [sbA(f"xt{i}", [128, D]) for i in range(2)]
    lnsq = sbA("lnsq", [128, D])
    lnt2 = sbA("lnt2", [128, D])
    xnbf = sbA("xnbf", [128, D], BF16)
    zt = [sbA(f"zt{i}", [128, CH]) for i in range(2)]
    spt = [sbA(f"spt{i}", [128, CH]) for i in range(4)]
    ekt = [sbA("ekt0", [128, CH])]
    ert = [sbA("ert0", [128, CH])]
    kst = [sbA(f"kst{i}", [128, CH], BF16) for i in range(2)]
    asm = [sbA(f"asm{i}", [128, D], BF16) for i in range(4)]
    eLx = sbA("eLx", [128, CH])
    qc = sbA("qc", [128, 16 * CH], BF16)

    xnf_v = xnf[:].rearrange("p (ct t) -> p ct t", ct=16)
    wblk_v = lambda i: wblk[i][:].rearrange("p (ct f) -> p ct f", ct=16)

    if "A" in phases:
        for tt in range(NT):
            xb, xkey = xt[tt % 2], ("xt", tt % 2)
            em.dma("sy", lambda e, tt=tt, xb=xb: e.dma_start(
                out=xb[:], in_=x_in[tt * 128:(tt + 1) * 128, :]), "xin", writes=[xkey])
            em.close_batch("xin")
            ln_tile(xb, xkey, lnt2, ("lnt2",))
            ln_tile(lnt2, ("lnt2",), lnt2, ("lnt2",))
            em.op("ve", lambda v: v.tensor_copy(xnbf[:], lnt2[:]),
                  reads=[("lnt2",)], writes=[("xnbf",)])
            for ft in range(16):
                pb = ft % 2
                em.op("te", lambda t, ft=ft, pb=pb: t.transpose(
                    pt[pb][:, 0:128], xnbf[:, ft * 128:(ft + 1) * 128], idbf[:]),
                    reads=[("xnbf",), ("idbf",)], writes=[("pt", pb)])
                em.op("se", lambda s_, ft=ft, pb=pb, tt=tt: s_.activation(
                    xnf_v[:, ft, tt * 128:(tt + 1) * 128], pt[pb][:, 0:128],
                    AF.Identity, bias=betf[:, ft:ft + 1], scale=gamf[:, ft:ft + 1]),
                    reads=[("pt", pb), ("gamf",), ("betf",)], writes=[("xnf",)])
        if "xnf" in dbg:
            em.dma("pe", lambda e: e.dma_start(
                out=dbg["xnf"].rearrange("(ct p) t -> p ct t", p=128), in_=xnf_v),
                "dbgp", reads=[("xnf",)])
            em.close_batch("dbgp")

    if "B" in phases:
        for fb in range(4):
            ws = fb % 2
            em.dma("pe", lambda e, fb=fb, ws=ws: e.dma_start(
                out=wblk_v(ws), in_=Wx[:, fb * 512:(fb + 1) * 512].rearrange("(ct p) f -> p ct f", p=128)),
                "wgt", writes=[("wblk", ws)])
            em.close_batch("wgt")
            for fi in range(4):
                ft = fb * 4 + fi
                for tc in range(NCH):
                    bank = (ft * NCH + tc) % 2
                    for ct in range(16):
                        em.op("te", lambda t, ct=ct, fi=fi, tc=tc, bank=bank, ws=ws: t.matmul(
                            ps[bank][:, :], wblk_v(ws)[:, ct, fi * 128:(fi + 1) * 128],
                            xnf_v[:, ct, tc * CH:(tc + 1) * CH], start=(ct == 0), stop=(ct == 15)),
                            reads=[("wblk", ws), ("xnf",)], writes=[("psb", bank)])
                    slot = bank
                    em.op("se", lambda s_, bank=bank, slot=slot: s_.activation(
                        kst[slot][:], ps[bank][:, :], AF.Identity),
                        reads=[("psb", bank)], writes=[("kst", slot)])
                    em.dma("sy", lambda e, ft=ft, tc=tc, slot=slot: e.dma_start(
                        out=q_s[ft * 128:(ft + 1) * 128, tc * CH:(tc + 1) * CH], in_=kst[slot][:]),
                        "qst", reads=[("kst", slot)], writes=[("qS",)])
                    em.close_batch("qst")

    if "C" in phases:
        em.op("ve", lambda v: v.memset(ltcc[:], 0.0), writes=[("ltc",)])
        for tc in range(NCH):
            em.dma("sy", lambda e, tc=tc: e.dma_start(
                out=qc[:].rearrange("p (ct t) -> p ct t", ct=16),
                in_=q_s[:, tc * CH:(tc + 1) * CH].rearrange("(ct p) t -> p ct t", p=128)),
                "qld", reads=[("qS",)], writes=[("qc",)])
            em.close_batch("qld")
            for fb in range(4):
                ws = fb % 2
                em.dma("pe", lambda e, fb=fb, ws=ws, tc=tc: e.dma_start(
                    out=wblk_v(ws), in_=Ww[:, fb * 512:(fb + 1) * 512].rearrange("(ct p) f -> p ct f", p=128)),
                    "wgt", writes=[("wblk", ws)])
                em.close_batch("wgt")
                for fi in range(4):
                    ft = fb * 4 + fi
                    bank = ft % 2
                    for ct in range(16):
                        em.op("te", lambda t, ct=ct, fi=fi, bank=bank, ws=ws: t.matmul(
                            ps[bank][:, :], wblk_v(ws)[:, ct, fi * 128:(fi + 1) * 128],
                            qc[:].rearrange("p (ct t) -> p ct t", ct=16)[:, ct, :],
                            start=(ct == 0), stop=(ct == 15)),
                            reads=[("wblk", ws), ("qc",)], writes=[("psb", bank)])
                    zb = zt[bank]
                    em.op("se", lambda s_, bank=bank, ft=ft, zb=zb: s_.activation(
                        zb[:], ps[bank][:, :], AF.Identity, bias=bwf[:, ft:ft + 1]),
                        reads=[("psb", bank), ("bwf",)], writes=[("zt", bank)])
                    em.op("se", lambda s_, zb=zb, bank=bank: s_.activation(spt[0][:], zb[:], AF.Abs),
                          reads=[("zt", bank)], writes=[("spt", 0)])
                    em.op("se", lambda s_: s_.activation(spt[1][:], spt[0][:], AF.Exp, scale=-1.0),
                          reads=[("spt", 0)], writes=[("spt", 1)])
                    em.op("se", lambda s_: s_.activation(spt[0][:], spt[1][:], AF.Ln, bias=1.0),
                          reads=[("spt", 1)], writes=[("spt", 0)])
                    em.op("ve", lambda v, zb=zb, bank=bank: v.tensor_scalar_max(spt[2][:], zb[:], 0.0),
                          reads=[("zt", bank)], writes=[("spt", 2)])
                    em.op("ve", lambda v: v.tensor_add(spt[3][:], spt[2][:], spt[0][:]),
                          reads=[("spt", 2), ("spt", 0)], writes=[("spt", 3)])
                    em.op("ve", lambda v: v.tensor_tensor_scan(
                        spt[2][:, ::-1], mseg[:], spt[3][:, ::-1], 0.0, ALU.mult, ALU.add),
                        reads=[("spt", 3), ("mseg",)], writes=[("spt", 2)])
                    em.op("ve", lambda v: v.tensor_sub(spt[2][:], spt[2][:], spt[3][:]),
                          reads=[("spt", 2), ("spt", 3)], writes=[("spt", 2)])
                    es = 0
                    em.op("se", lambda s_, es=es: s_.activation(ekt[es][:], spt[2][:], AF.Exp, scale=-1.0),
                          reads=[("spt", 2)], writes=[("ekt", es)])
                    em.op("se", lambda s_, es=es: s_.activation(ert[es][:], spt[2][:], AF.Exp),
                          reads=[("spt", 2)], writes=[("ert", es)])
                    em.dma("sy", lambda e, ft=ft, tc=tc, es=es: e.dma_start(
                        out=ek_s[ft * 128:(ft + 1) * 128, tc * CH:(tc + 1) * CH], in_=ekt[es][:]),
                        "ekst", reads=[("ekt", es)], writes=[("ekS",)])
                    em.dma("sy", lambda e, ft=ft, tc=tc, es=es: e.dma_start(
                        out=er_s[ft * 128:(ft + 1) * 128, tc * CH:(tc + 1) * CH], in_=ert[es][:]),
                        "ekst", reads=[("ert", es)], writes=[("ekS",)])
                    em.close_batch("ekst")
                    c0 = ft * NM + tc * MPC
                    em.op("ve", lambda v, c0=c0: v.tensor_tensor(
                        eTC[:, c0:c0 + MPC], spt[2][:, 0::L], spt[3][:, 0::L], ALU.add),
                        reads=[("spt", 2), ("spt", 3)], writes=[("eTC", c0)])
                    em.op("ve", lambda v, c0=c0, ft=ft: v.tensor_tensor_scan(
                        eLs[:, c0:c0 + MPC], mseg[:, 1:1 + MPC], eTC[:, c0:c0 + MPC],
                        ltcc[:, ft:ft + 1], ALU.mult, ALU.add),
                        reads=[("eTC", c0), ("ltc",), ("mseg",)], writes=[("eL", c0)])
                    em.op("ve", lambda v, c0=c0, ft=ft: v.tensor_copy(
                        ltcc[:, ft:ft + 1], eLs[:, c0 + MPC - 1:c0 + MPC]),
                        reads=[("eL", c0)], writes=[("ltc",)])
                    em.op("se", lambda s_, c0=c0: s_.activation(
                        eTC[:, c0:c0 + MPC], eTC[:, c0:c0 + MPC], AF.Exp, scale=-1.0),
                        reads=[("eTC", c0)], writes=[("eTC", c0)])
                    em.op("se", lambda s_, c0=c0: s_.activation(
                        eLs[:, c0:c0 + MPC], eLs[:, c0:c0 + MPC], AF.Exp, scale=-1.0),
                        reads=[("eL", c0)], writes=[("eL", c0)])
        em.op("ve", lambda v: v.tensor_copy(DsumB[:], ltcc[:]), reads=[("ltc",)], writes=[("Dsum",)])

    if "D" in phases:
        for tc in range(NCH):
            for fb in range(4):
                ws = fb % 2
                em.dma("pe", lambda e, fb=fb, ws=ws, tc=tc: e.dma_start(
                    out=wblk_v(ws), in_=Wk[:, fb * 512:(fb + 1) * 512].rearrange("(ct p) f -> p ct f", p=128)),
                    "wgt", writes=[("wblk", ws)])
                em.close_batch("wgt")
                for fi in range(4):
                    ft = fb * 4 + fi
                    bank = ft % 2
                    es = 0
                    em.dma("sy", lambda e, ft=ft, tc=tc, es=es: e.dma_start(
                        out=ekt[es][:], in_=ek_s[ft * 128:(ft + 1) * 128, tc * CH:(tc + 1) * CH]),
                        "ekld", reads=[("ekS",)], writes=[("ekt", es)])
                    em.close_batch("ekld")
                    for ct in range(16):
                        em.op("te", lambda t, ct=ct, fi=fi, tc=tc, bank=bank, ws=ws: t.matmul(
                            ps[bank][:, :], wblk_v(ws)[:, ct, fi * 128:(fi + 1) * 128],
                            xnf_v[:, ct, tc * CH:(tc + 1) * CH], start=(ct == 0), stop=(ct == 15)),
                            reads=[("wblk", ws), ("xnf",)], writes=[("psb", bank)])
                    em.op("ve", lambda v, bank=bank, es=es: v.tensor_tensor(
                        kst[bank][:], ps[bank][:, :], ekt[es][:], ALU.mult),
                        reads=[("psb", bank), ("ekt", es)], writes=[("kst", bank)])
                    em.dma("sy", lambda e, ft=ft, tc=tc, bank=bank: e.dma_start(
                        out=ktf_s[ft * 128:(ft + 1) * 128, tc * CH:(tc + 1) * CH], in_=kst[bank][:]),
                        "kfst", reads=[("kst", bank)], writes=[("kfS",)])
                    em.close_batch("kfst")
                    for g in range(4):
                        pb = g % 2
                        em.op("te", lambda t, g=g, pb=pb, bank=bank: t.transpose(
                            pt[pb][:, 0:128], kst[bank][:, g * 128:(g + 1) * 128], idbf[:]),
                            reads=[("kst", bank), ("idbf",)], writes=[("pt", pb)])
                        em.op("se", lambda s_, g=g, pb=pb, ft=ft: s_.activation(
                            asm[g][:, ft * 128:(ft + 1) * 128], pt[pb][:, 0:128], AF.Identity),
                            reads=[("pt", pb)], writes=[("asm", g, ft)])
            for g in range(4):
                em.dma("sy", lambda e, tc=tc, g=g: e.dma_start(
                    out=ktt_s[tc * CH + g * 128:tc * CH + (g + 1) * 128, :], in_=asm[g][:]),
                    "ktst", reads=[("asm", g, ftx) for ftx in range(16)], writes=[("ktS",)])
            em.close_batch("ktst")

    if "E" in phases:
        for ttg in range(NT // 4):
            for jb in range(4):
                ws = jb % 2
                em.dma("pe", lambda e, jb=jb, ws=ws, ttg=ttg: e.dma_start(
                    out=wblk_v(ws), in_=Wv[:, jb * 512:(jb + 1) * 512].rearrange("(ct p) f -> p ct f", p=128)),
                    "wgt", writes=[("wblk", ws)])
                em.close_batch("wgt")
                for tl in range(4):
                    tt = ttg * 4 + tl
                    bank = (jb * 4 + tl) % 2
                    for ct in range(16):
                        em.op("te", lambda t, ct=ct, tt=tt, bank=bank, ws=ws: t.matmul(
                            ps[bank][:, :], xnf_v[:, ct, tt * 128:(tt + 1) * 128],
                            wblk_v(ws)[:, ct, 0:512], start=(ct == 0), stop=(ct == 15)),
                            reads=[("wblk", ws), ("xnf",)], writes=[("psb", bank)])
                    em.op("ve", lambda v, tl=tl, jb=jb, bank=bank: v.tensor_copy(
                        asm[tl][:, jb * 512:(jb + 1) * 512], ps[bank][:, :]),
                        reads=[("psb", bank)], writes=[("asm", tl, jb)])
            for tl in range(4):
                tt = ttg * 4 + tl
                em.dma("sy", lambda e, tt=tt, tl=tl: e.dma_start(
                    out=vtt_s[tt * 128:(tt + 1) * 128, :], in_=asm[tl][:]),
                    "vst", reads=[("asm", tl, jbx) for jbx in range(4)], writes=[("vtS",)])
            em.close_batch("vst")

    if "F" in phases:
        for fb in range(4):
            ws = fb % 2
            em.dma("pe", lambda e, fb=fb, ws=ws: e.dma_start(
                out=wblk_v(ws), in_=Wr[:, fb * 512:(fb + 1) * 512].rearrange("(ct p) f -> p ct f", p=128)),
                "wgt", writes=[("wblk", ws)])
            em.close_batch("wgt")
            for fi in range(4):
                ft = fb * 4 + fi
                for tc in range(NCH):
                    bank = (ft * NCH + tc) % 2
                    es = 0
                    em.dma("sy", lambda e, ft=ft, tc=tc, es=es: e.dma_start(
                        out=ert[es][:], in_=er_s[ft * 128:(ft + 1) * 128, tc * CH:(tc + 1) * CH]),
                        "ekld", reads=[("ekS",)], writes=[("ert", es)])
                    em.close_batch("ekld")
                    for ct in range(16):
                        em.op("te", lambda t, ct=ct, fi=fi, tc=tc, bank=bank, ws=ws: t.matmul(
                            ps[bank][:, :], wblk_v(ws)[:, ct, fi * 128:(fi + 1) * 128],
                            xnf_v[:, ct, tc * CH:(tc + 1) * CH], start=(ct == 0), stop=(ct == 15)),
                            reads=[("wblk", ws), ("xnf",)], writes=[("psb", bank)])
                    em.op("se", lambda s_, bank=bank: s_.activation(spt[0][:], ps[bank][:, :], AF.Sigmoid),
                          reads=[("psb", bank)], writes=[("spt", 0)])
                    em.op("ve", lambda v, bank=bank, es=es: v.tensor_tensor(
                        kst[bank][:], spt[0][:], ert[es][:], ALU.mult),
                        reads=[("spt", 0), ("ert", es)], writes=[("kst", bank)])
                    em.dma("sy", lambda e, ft=ft, tc=tc, bank=bank: e.dma_start(
                        out=rtf_s[ft * 128:(ft + 1) * 128, tc * CH:(tc + 1) * CH], in_=kst[bank][:]),
                        "rfst", reads=[("kst", bank)], writes=[("rfS",)])
                    em.close_batch("rfst")
                    c0 = ft * NM + tc * MPC
                    em.op("ve", lambda v, c0=c0: v.tensor_copy(
                        eLx[:].rearrange("p (m i) -> p m i", m=MPC),
                        eLs[:, c0:c0 + MPC, None].broadcast_to([128, MPC, L])),
                        reads=[("eL", c0)], writes=[("eLx",)])
                    em.op("ve", lambda v, bank=bank: v.tensor_tensor(
                        kst[1 - bank][:], kst[bank][:], eLx[:], ALU.mult),
                        reads=[("kst", bank), ("eLx",)], writes=[("kst", 1 - bank)])
                    em.dma("sy", lambda e, ft=ft, tc=tc, bank=bank: e.dma_start(
                        out=rrf_s[ft * 128:(ft + 1) * 128, tc * CH:(tc + 1) * CH], in_=kst[1 - bank][:]),
                        "rrst", reads=[("kst", 1 - bank)], writes=[("rrS",)])
                    em.close_batch("rrst")

    em.barrier()
    poolA.close()

    poolB = ExitStack()
    sbB = lambda name, shape, dt=F32: poolB.enter_context(nc.sbuf_tensor(name, shape, dt))
    kf_c = [sbB(f"kf_c{i}", [128, 16 * CH], BF16) for i in range(2)]
    rf_c = [sbB(f"rf_c{i}", [128, 16 * CH], BF16) for i in range(2)]
    kt_c = [sbB(f"kt_c{i}", [128, (CH // 128) * D], BF16) for i in range(2)]
    vt_c = [sbB(f"vt_c{i}", [128, (CH // 128) * D], BF16) for i in range(2)]
    kstg = [sbB("kstg0", [32, D], BF16)]
    vstg = [sbB("vstg0", [32, D], BF16)]
    scm = sbB("scm", [128, 2048], BF16)
    SdecB = sbB("SdecB", [128, NPAIR * HS])
    SdecBF = sbB("SdecBF", [128, NPAIR * HS], BF16)
    ysb = [sbB(f"ysb{i}", [128, 16 * CH], BF16) for i in range(2)]
    ccst = sbB("ccst", [128, CCW])
    drm = sbB("drm", [128, 16])
    SgB = SdecB

    if "G" in phases:
        em.op("ve", lambda v: v.memset(SpackB[:], 0.0), writes=[("Spack",)])

        def g_loads(tc):
            sl = tc % 2
            em.dma("sy", lambda e, tc=tc, sl=sl: e.dma_start(
                out=kf_c[sl][:].rearrange("p (ct t) -> p ct t", ct=16),
                in_=ktf_s[:, tc * CH:(tc + 1) * CH].rearrange("(ct p) t -> p ct t", p=128)),
                "gld", reads=[("kfS",)], writes=[("kf_c", sl)])
            em.dma("sy", lambda e, tc=tc, sl=sl: e.dma_start(
                out=rf_c[sl][:].rearrange("p (ct t) -> p ct t", ct=16),
                in_=rtf_s[:, tc * CH:(tc + 1) * CH].rearrange("(ct p) t -> p ct t", p=128)),
                "gld", reads=[("rfS",)], writes=[("rf_c", sl)])
            em.dma("sy", lambda e, tc=tc, sl=sl: e.dma_start(
                out=kt_c[sl][:].rearrange("p (g f) -> p g f", g=CH // 128),
                in_=ktt_s[tc * CH:(tc + 1) * CH, :].rearrange("(g p) f -> p g f", p=128)),
                "gld", reads=[("ktS",)], writes=[("kt_c", sl)])
            em.dma("sy", lambda e, tc=tc, sl=sl: e.dma_start(
                out=vt_c[sl][:].rearrange("p (g f) -> p g f", g=CH // 128),
                in_=vtt_s[tc * CH:(tc + 1) * CH, :].rearrange("(g p) f -> p g f", p=128)),
                "gld", reads=[("vtS",)], writes=[("vt_c", sl)])
            em.close_batch("gld")

        g_loads(0)
        for tc in range(NCH):
            sl = tc % 2
            if tc + 1 < NCH:
                g_loads(tc + 1)
            kfv = kf_c[sl][:].rearrange("p (ct t) -> p ct t", ct=16)
            rfv = rf_c[sl][:].rearrange("p (ct t) -> p ct t", ct=16)
            ktv = kt_c[sl][:].rearrange("p (g f) -> p g f", g=CH // 128)
            vtv = vt_c[sl][:].rearrange("p (g f) -> p g f", g=CH // 128)

            def em_stage(g, sl=sl):
                em.dma("pe", lambda e, g=g, sl=sl: e.dma_start(
                    out=kstg[0][0:32, :], in_=kt_c[sl][96:128, g * D:(g + 1) * D]),
                    "stg", reads=[("kt_c", sl)], writes=[("kstg", 0)])
                em.dma("pe", lambda e, g=g, sl=sl: e.dma_start(
                    out=vstg[0][0:32, :], in_=vt_c[sl][96:128, g * D:(g + 1) * D]),
                    "stg", reads=[("vt_c", sl)], writes=[("vstg", 0)])
                em.close_batch("stg")

            def em_sdec(m, sl=sl, tc=tc):
                mg = tc * MPC + m
                em.op("ve", lambda v, mg=mg: v.tensor_copy(
                    ccst[:, 0:1024].rearrange("p (pr j) -> p pr j", pr=16),
                    eTC[:, mg::NM][:, :, None].broadcast_to([128, 16, HS])),
                    reads=[("eTC", prx * NM + tc * MPC) for prx in range(NPAIR)],
                    writes=[("ccst",)])
                em.op("ve", lambda v: v.tensor_mul(SdecB[:], SpackB[:], ccst[:, 0:1024]),
                      reads=[("Spack",), ("ccst",)], writes=[("SdecA",)])
                em.op("se", lambda s_: s_.activation(SdecBF[:], SdecB[:], AF.Identity),
                      reads=[("SdecA",)], writes=[("SdecBF",)])

            def em_scores(m, sl=sl, kfv=kfv, rfv=rfv):
                b = 32 * (m % 4) if (m % 4) != 3 else 0
                for h in range(HEADS):
                    hb = 64 * (h % 2)
                    prh = h // 2
                    bank = ps[h % 2]
                    col = (h // 2) * 32
                    em.op("te", lambda t, hb=hb, prh=prh, bank=bank, col=col, m=m, b=b: t.matmul(
                        bank[b:b + 32, col:col + 32],
                        kfv[hb:hb + 64, prh, m * L:(m + 1) * L],
                        rfv[hb:hb + 64, prh, m * L:(m + 1) * L],
                        start=True, stop=True),
                        reads=[("kf_c", sl), ("rf_c", sl)], writes=[("psb", h % 2)])

            def em_masks(m):
                b = 32 * (m % 4) if (m % 4) != 3 else 0
                for half in range(2):
                    em.op("ve", lambda v, half=half, b=b, m=m: v.tensor_tensor(
                        scm[b:b + 32, (m % 2) * 1024 + half * 512:(m % 2) * 1024 + (half + 1) * 512],
                        ps[half][b:b + 32, 0:512], mrepb[b:b + 32, :], ALU.mult),
                        reads=[("psb", half), ("mrepb",)], writes=[("scm", m % 2, half)])

            def em_y(m, sl=sl, vtv=vtv, rfv=rfv):
                g = m // 4
                b = 32 * (m % 4) if (m % 4) != 3 else 0
                for h in range(HEADS):
                    hb = 64 * (h % 2)
                    prh = h // 2
                    half = h % 2
                    scol = (m % 2) * 1024 + half * 512 + (h // 2) * 32
                    if m % 4 == 3:
                        vsrc = vstg[0][0:32, h * HS:(h + 1) * HS]
                        vread = ("vstg", 0)
                    else:
                        vsrc = vtv[b:b + 32, g, h * HS:(h + 1) * HS]
                        vread = ("vt_c", sl)
                    em.op("te", lambda t, vsrc=vsrc, scol=scol, hb=hb, prh=prh, b=b: t.matmul(
                        ps[2][hb:hb + 64, prh * 32:(prh + 1) * 32],
                        vsrc, scm[b:b + 32, scol:scol + 32],
                        start=True, stop=True),
                        reads=[vread, ("scm", m % 2, half)], writes=[("psb", 2)])
                    em.op("te", lambda t, hb=hb, prh=prh, m=m: t.matmul(
                        ps[5][hb:hb + 64, prh * 32:(prh + 1) * 32],
                        SdecBF[hb:hb + 64, prh * HS:(prh + 1) * HS],
                        rfv[hb:hb + 64, prh, m * L:(m + 1) * L],
                        start=True, stop=True),
                        reads=[("SdecBF",), ("rf_c", sl)], writes=[("psb", 5)])

            def em_sadd(m, sl=sl, ktv=ktv, vtv=vtv):
                g = m // 4
                b = 32 * (m % 4) if (m % 4) != 3 else 0
                for h in range(HEADS):
                    hb = 64 * (h % 2)
                    prh = h // 2
                    bank = 3 if prh < 8 else 4
                    col = (prh % 8) * HS
                    if m % 4 == 3:
                        ksrc = kstg[0][0:32, h * HS:(h + 1) * HS]
                        vsrc = vstg[0][0:32, h * HS:(h + 1) * HS]
                        kread, vread = ("kstg", 0), ("vstg", 0)
                    else:
                        ksrc = ktv[b:b + 32, g, h * HS:(h + 1) * HS]
                        vsrc = vtv[b:b + 32, g, h * HS:(h + 1) * HS]
                        kread, vread = ("kt_c", sl), ("vt_c", sl)
                    em.op("te", lambda t, ksrc=ksrc, vsrc=vsrc, hb=hb, col=col, bank=bank: t.matmul(
                        ps[bank][hb:hb + 64, col:col + HS], ksrc, vsrc,
                        start=True, stop=True),
                        reads=[kread, vread], writes=[("psb", bank)])

            def em_supd(m):
                em.op("ve", lambda v: v.tensor_add(SpackB[:, 0:512], SdecB[:, 0:512], ps[3][:, :]),
                      reads=[("SdecA",), ("psb", 3)], writes=[("Spack",)])
                em.op("ve", lambda v: v.tensor_add(SpackB[:, 512:1024], SdecB[:, 512:1024], ps[4][:, :]),
                      reads=[("SdecA",), ("psb", 4)], writes=[("Spack",)])

            def em_yevac(m, sl=sl):
                em.op("se", lambda s_, m=m: s_.activation(
                    ysb[sl][:].rearrange("p (pr t) -> p pr t", pr=16)[:, :, m * 32:(m + 1) * 32],
                    ps[2][:, :].rearrange("p (pr i) -> p pr i", pr=16), AF.Identity),
                    reads=[("psb", 2)], writes=[("ysb", sl, m)])
                em.op("ve", lambda v, m=m: v.tensor_tensor(
                    ysb[sl][:].rearrange("p (pr t) -> p pr t", pr=16)[:, :, m * 32:(m + 1) * 32],
                    ysb[sl][:].rearrange("p (pr t) -> p pr t", pr=16)[:, :, m * 32:(m + 1) * 32],
                    ps[5][:, :].rearrange("p (pr i) -> p pr i", pr=16), ALU.add),
                    reads=[("psb", 5), ("ysb", sl, m)], writes=[("ysb", sl, m)])

            em_stage(0)
            em_sdec(0)
            em_scores(0)
            em_masks(0)
            for m in range(MPC):
                if m % 4 == 0 and m > 0:
                    em_stage(m // 4)
                em_y(m)
                if m + 1 < MPC:
                    em_scores(m + 1)
                em_sadd(m)
                em_supd(m)
                em_yevac(m)
                if m + 1 < MPC:
                    em_masks(m + 1)
                    em_sdec(m + 1)
            em.dma("sy", lambda e, tc=tc, sl=sl: e.dma_start(
                out=y_s[:, tc * CH:(tc + 1) * CH].rearrange("(pr p) t -> p pr t", p=128),
                in_=ysb[sl][:].rearrange("p (pr t) -> p pr t", pr=16)),
                "yst", reads=[("ysb", sl, mx) for mx in range(MPC)], writes=[("yS",)])
            em.close_batch("yst")

    if "H" in phases:
        em.op("ve", lambda v: v.tensor_copy(ccst[:, 0:NPAIR * HS], SpackB[:]),
              reads=[("Spack",)], writes=[("ccst",)])
        em.op("ve", lambda v: v.tensor_copy(ccst[:, NPAIR * HS:CCW], DsumB[:]),
              reads=[("Dsum",)], writes=[("ccst",)])
        em.dma("pe", lambda e: e.dma_start(out=cc_in[:, :], in_=ccst[:]),
               "ccw", reads=[("ccst",)], writes=[("ccin",)])
        em.close_batch("ccw")
        em.op("pe", lambda g: g.collective_compute(
            "AllGather", ALU.bypass, replica_groups=[list(range(NCORE))],
            ins=[cc_in.ap().opt()], outs=[cc_out.ap().opt()]),
            reads=[("ccin",)], writes=[("ccout",)])
        em.op("ve", lambda v: v.memset(SinB[:], 0.0), writes=[("Sin",)])
        for r in range(NCORE - 1):
            em.dma("sy", lambda e, r=r: e.dma_start(
                out=ccst[:], in_=cc_out.ap()[r * 128:(r + 1) * 128, :]),
                "ccr", reads=[("ccout",)], writes=[("ccst",)])
            em.close_batch("ccr")
            em.op("ve", lambda v, r=r: v.tensor_scalar_mul(
                drm[:], ccst[:, NPAIR * HS:CCW], gmask[:, r:r + 1]),
                reads=[("ccst",), ("gmask",)], writes=[("drm",)])
            em.op("se", lambda s_: s_.activation(drm[:], drm[:], AF.Exp, scale=-1.0),
                  reads=[("drm",)], writes=[("drm",)])
            em.op("ve", lambda v, r=r: v.tensor_scalar_mul(
                SgB[:], ccst[:, 0:NPAIR * HS], gmask[:, r:r + 1]),
                reads=[("ccst",), ("gmask",)], writes=[("SdecA",)])
            for pr in range(NPAIR):
                em.op("ve", lambda v, pr=pr: v.scalar_tensor_tensor(
                    SinB[:, pr * HS:(pr + 1) * HS], SinB[:, pr * HS:(pr + 1) * HS],
                    drm[:, pr:pr + 1], SgB[:, pr * HS:(pr + 1) * HS], ALU.mult, ALU.add),
                    reads=[("Sin",), ("drm",), ("SdecA",)], writes=[("Sin",)])
        em.op("ve", lambda v: v.tensor_copy(SinBF[:], SinB[:]),
              reads=[("Sin",)], writes=[("SinBF",)])

    em.barrier()
    poolB.close()

    poolC = ExitStack()
    sbC = lambda name, shape, dt=F32: poolC.enter_context(nc.sbuf_tensor(name, shape, dt))
    yfx = [sbC(f"yfx{i}", [128, 16 * CH], BF16) for i in range(2)]
    rrc = [sbC(f"rrc{i}", [128, 16 * CH], BF16) for i in range(2)]
    ybuf = [sbC(f"ybuf{i}", [128, 16 * CH], BF16) for i in range(2)]
    wblkC = [sbC(f"wblkC{i}", [128, 16 * 512], BF16) for i in range(2)]
    xr = [sbC(f"xr{i}", [128, 512]) for i in range(2)]
    h1p = [sbC(f"h1p{i}", [128, 512]) for i in range(2)]
    h1row = [sbC(f"h1row{i}", [128, D]) for i in range(2)]
    lnsqC = sbC("lnsqC", [128, D])
    hnb = sbC("hnb", [128, D], BF16)
    hfstg = [sbC(f"hfstg{i}", [128, 16 * 128], BF16) for i in range(2)]
    wblkC_v = lambda i: wblkC[i][:].rearrange("p (ct f) -> p ct f", ct=16)

    if "I" in phases:
        def ln_tileC(src, srckey, dst, dstkey):
            em.op("se", lambda s_: s_.activation(lnsqC[:], src[:, 0:D], AF.Square,
                                                 accum_out=st8[:, 1:2]),
                  reads=[srckey], writes=[("lnsqC",), ("st8", 1)])
            em.op("ve", lambda v: v.reduce_sum(st8[:, 0:1], src[:, 0:D], axis=AX.X),
                  reads=[srckey], writes=[("st8", 0)])
            em.op("ve", lambda v: v.tensor_scalar_mul(st8[:, 0:1], st8[:, 0:1], 1.0 / D),
                  reads=[("st8", 0)], writes=[("st8", 0)])
            em.op("ve", lambda v: v.tensor_scalar(st8[:, 2:3], st8[:, 0:1], st8[:, 0:1],
                                                  -1.0, ALU.mult, ALU.mult),
                  reads=[("st8", 0)], writes=[("st8", 2)])
            em.op("ve", lambda v: v.scalar_tensor_tensor(st8[:, 1:2], st8[:, 1:2], 1.0 / D,
                                                         st8[:, 2:3], ALU.mult, ALU.add),
                  reads=[("st8", 1), ("st8", 2)], writes=[("st8", 1)])
            em.op("ve", lambda v: v.tensor_scalar_add(st8[:, 1:2], st8[:, 1:2], float(EPS)),
                  reads=[("st8", 1)], writes=[("st8", 1)])
            em.op("se", lambda s_: s_.activation(st8[:, 1:2], st8[:, 1:2], AF.Ln),
                  reads=[("st8", 1)], writes=[("st8", 1)])
            em.op("se", lambda s_: s_.activation(st8[:, 1:2], st8[:, 1:2], AF.Exp, scale=-0.5),
                  reads=[("st8", 1)], writes=[("st8", 1)])
            em.op("ve", lambda v: v.tensor_scalar(st8[:, 3:4], st8[:, 0:1], st8[:, 1:2],
                                                  -1.0, ALU.mult, ALU.mult),
                  reads=[("st8", 0), ("st8", 1)], writes=[("st8", 3)])
            em.op("se", lambda s_: s_.activation(dst[:, 0:D], src[:, 0:D], AF.Identity,
                                                 bias=st8[:, 3:4], scale=st8[:, 1:2]),
                  reads=[srckey, ("st8", 1), ("st8", 3)], writes=[dstkey])

        for tc in range(NCH):
            sl = tc % 2
            em.dma("sy", lambda e, tc=tc, sl=sl: e.dma_start(
                out=ybuf[sl][:].rearrange("p (pr t) -> p pr t", pr=16),
                in_=y_s[:, tc * CH:(tc + 1) * CH].rearrange("(pr p) t -> p pr t", p=128)),
                "ijld", reads=[("yS",)], writes=[("ybuf", sl)])
            em.dma("sy", lambda e, tc=tc, sl=sl: e.dma_start(
                out=rrc[sl][:].rearrange("p (pr t) -> p pr t", pr=16),
                in_=rrf_s[:, tc * CH:(tc + 1) * CH].rearrange("(pr p) t -> p pr t", p=128)),
                "ijld", reads=[("rrS",)], writes=[("rrc", sl)])
            em.close_batch("ijld")
            for pr in range(NPAIR):
                bank = pr % 2
                em.op("te", lambda t, pr=pr, bank=bank, sl=sl: t.matmul(
                    ps[bank][0:64, :], SinBF[0:64, pr * HS:(pr + 1) * HS],
                    rrc[sl][:].rearrange("p (q t) -> p q t", q=16)[0:64, pr, :],
                    start=True, stop=True),
                    reads=[("SinBF",), ("rrc", sl)], writes=[("psb", bank)])
                em.op("te", lambda t, pr=pr, bank=bank, sl=sl: t.matmul(
                    ps[bank][64:128, :], SinBF[64:128, pr * HS:(pr + 1) * HS],
                    rrc[sl][:].rearrange("p (q t) -> p q t", q=16)[64:128, pr, :],
                    start=True, stop=True),
                    reads=[("SinBF",), ("rrc", sl)], writes=[("psb", bank)])
                em.op("ve", lambda v, pr=pr, bank=bank, sl=sl: v.tensor_tensor(
                    yfx[sl][:].rearrange("p (q t) -> p q t", q=16)[:, pr, :],
                    ybuf[sl][:].rearrange("p (q t) -> p q t", q=16)[:, pr, :],
                    ps[bank][:, :], ALU.add),
                    reads=[("ybuf", sl), ("psb", bank)], writes=[("yfx", sl, pr)])
            for fb in range(4):
                ws = fb % 2
                em.dma("pe", lambda e, fb=fb, ws=ws, tc=tc: e.dma_start(
                    out=wblkC_v(ws), in_=Wo[:, fb * 512:(fb + 1) * 512].rearrange("(ct p) f -> p ct f", p=128)),
                    "wgt", writes=[("wblkC", ws)])
                em.close_batch("wgt")
                for tl in range(4):
                    bank = 2 + (fb * 4 + tl) % 2
                    xs_ = (fb * 4 + tl) % 2
                    em.dma("sy", lambda e, tc=tc, tl=tl, fb=fb, xs_=xs_: e.dma_start(
                        out=xr[xs_][:], in_=x_in[tc * CH + tl * 128:tc * CH + (tl + 1) * 128,
                                                 fb * 512:(fb + 1) * 512]),
                        "xrld", writes=[("xr", xs_)])
                    em.close_batch("xrld")
                    for jt in range(16):
                        em.op("te", lambda t, jt=jt, tl=tl, bank=bank, ws=ws, sl=sl: t.matmul(
                            ps[bank][:, :], yfx[sl][:].rearrange("p (q t) -> p q t", q=16)[:, jt, tl * 128:(tl + 1) * 128],
                            wblkC_v(ws)[:, jt, 0:512], start=(jt == 0), stop=(jt == 15)),
                            reads=[("yfx", sl, jt), ("wblkC", ws)], writes=[("psb", bank)])
                    hp = (fb * 4 + tl) % 2
                    em.op("ve", lambda v, bank=bank, hp=hp, xs_=xs_: v.tensor_tensor(
                        h1p[hp][:], ps[bank][:, :], xr[xs_][:], ALU.add),
                        reads=[("psb", bank), ("xr", xs_)], writes=[("h1p", hp)])
                    em.dma("sy", lambda e, tc=tc, tl=tl, fb=fb, hp=hp: e.dma_start(
                        out=h1_s[tc * CH + tl * 128:tc * CH + (tl + 1) * 128,
                                 fb * 512:(fb + 1) * 512], in_=h1p[hp][:]),
                        "h1st", reads=[("h1p", hp)], writes=[("h1S",)])
                    em.close_batch("h1st")
            for tl in range(4):
                hs_ = tl % 2
                em.dma("sy", lambda e, tc=tc, tl=tl, hs_=hs_: e.dma_start(
                    out=h1row[hs_][:], in_=h1_s[tc * CH + tl * 128:tc * CH + (tl + 1) * 128, :]),
                    "h1rl", reads=[("h1S",)], writes=[("h1row", hs_)])
                em.close_batch("h1rl")
                ln_tileC(h1row[hs_], ("h1row", hs_), h1row[hs_], ("h1row", hs_))
                em.op("ve", lambda v, hs_=hs_: v.tensor_copy(hnb[:], h1row[hs_][:]),
                      reads=[("h1row", hs_)], writes=[("hnb",)])
                for ft in range(16):
                    pb = ft % 2
                    em.op("te", lambda t, ft=ft, pb=pb: t.transpose(
                        pt[pb][:, 0:128], hnb[:, ft * 128:(ft + 1) * 128], idbf[:]),
                        reads=[("hnb",), ("idbf",)], writes=[("pt", pb)])
                    fsl = tl % 2
                    em.op("se", lambda s_, ft=ft, pb=pb, fsl=fsl: s_.activation(
                        hfstg[fsl][:, ft * 128:(ft + 1) * 128], pt[pb][:, 0:128], AF.Identity),
                        reads=[("pt", pb)], writes=[("hfstg", fsl, ft)])
                em.dma("sy", lambda e, tc=tc, tl=tl: e.dma_start(
                    out=hf_s[:, tc * CH + tl * 128:tc * CH + (tl + 1) * 128].rearrange("(ft p) t -> p ft t", p=128),
                    in_=hfstg[tl % 2][:].rearrange("p (ft t) -> p ft t", ft=16)),
                    "hfst", reads=[("hfstg", tl % 2, ftx) for ftx in range(16)], writes=[("hfS",)])
                em.close_batch("hfst")

    em.barrier()
    poolC.close()

    poolD = ExitStack()
    sbD = lambda name, shape, dt=F32: poolD.enter_context(nc.sbuf_tensor(name, shape, dt))
    hfc = [sbD(f"hfc{i}", [128, 16 * CH], BF16) for i in range(2)]
    wgu = [sbD(f"wgu{i}", [128, 16 * (DI // 2)], BF16) for i in range(4)]
    wdd = [sbD(f"wdd{i}", [128, 6 * 512], BF16) for i in range(2)]
    hexp = sbD("hexp", [128, 6 * TB], BF16)
    accs = sbD("accs", [128, NTTB * D])
    sgs = sbD("sgs", [128, 2 * 512], BF16)
    rwT = sbD("rwT", [128, 16 * NE], BF16)
    rwstage = sbD("rwstage", [32, D], BF16)
    wde = sbD("wde", [128, NTTB * NE])
    rst = sbD("rst", [128, 16])
    lgt = sbD("lgt", [128, 2 * NE])
    h1tD = sbD("h1tD", [128, D])

    if "K" in phases:
        em.dma("pe", lambda e: e.dma_start(out=rwstage[0:32, :], in_=rweights),
               "rwl", writes=[("rwstage",)])
        em.close_batch("rwl")
        for ct in range(16):
            pb = ct % 2
            em.op("te", lambda t, ct=ct, pb=pb: t.transpose(
                pt[pb][:, 0:32], rwstage[0:32, ct * 128:(ct + 1) * 128], idbf[0:32, 0:32]),
                reads=[("rwstage",), ("idbf",)], writes=[("pt", pb)])
            em.op("se", lambda s_, ct=ct, pb=pb: s_.activation(
                rwT[:, ct * NE:(ct + 1) * NE], pt[pb][:, 0:32], AF.Identity),
                reads=[("pt", pb)], writes=[("rwT",)])

        GRPB = NCHB * 6
        DWNB = NTTB * 4
        for blk_i in range(NB):
            for tch in range(NCHB):
                sl = tch
                em.dma("sy", lambda e, blk_i=blk_i, tch=tch, sl=sl: e.dma_start(
                    out=hfc[sl][:].rearrange("p (ct t) -> p ct t", ct=16),
                    in_=hf_s[:, blk_i * TB + tch * CH:blk_i * TB + (tch + 1) * CH].rearrange(
                        "(ct p) t -> p ct t", p=128)),
                    "hfl", reads=[("hfS",)], writes=[("hfc", sl)])
                em.close_batch("hfl")
            em.op("ve", lambda v: v.memset(accs[:], 0.0), writes=[("accs",)])
            for tch in range(NCHB):
                for tl in range(4):
                    ttg = tch * 4 + tl
                    bank = 5
                    for ct in range(16):
                        em.op("te", lambda t, ct=ct, tch=tch, tl=tl, bank=bank: t.matmul(
                            ps[bank][:, 0:NE],
                            hfc[tch][:].rearrange("p (c t) -> p c t", c=16)[:, ct, tl * 128:(tl + 1) * 128],
                            rwT[:, ct * NE:(ct + 1) * NE], start=(ct == 0), stop=(ct == 15)),
                            reads=[("hfc", tch), ("rwT",)], writes=[("psb", bank)])
                    em.op("ve", lambda v, bank=bank: v.tensor_copy(lgt[:, 0:NE], ps[bank][:, 0:NE]),
                          reads=[("psb", bank)], writes=[("lgt", 0)])
                    em.op("ve", lambda v: v.reduce_max(rst[:, 0:1], lgt[:, 0:NE], axis=AX.X),
                          reads=[("lgt", 0)], writes=[("rst", 0)])
                    em.op("ve", lambda v: v.tensor_scalar_mul(rst[:, 1:2], rst[:, 0:1], -1.0),
                          reads=[("rst", 0)], writes=[("rst", 1)])
                    em.op("se", lambda s_: s_.activation(lgt[:, NE:2 * NE], lgt[:, 0:NE], AF.Exp,
                                                         bias=rst[:, 1:2]),
                          reads=[("lgt", 0), ("rst", 1)], writes=[("lgt", 1)])
                    em.op("ve", lambda v: v.max(rst[:, 8:16], lgt[:, NE:2 * NE]),
                          reads=[("lgt", 1)], writes=[("rst", 8)])
                    em.op("ve", lambda v: v.reduce_sum(rst[:, 2:3], rst[:, 8:16], axis=AX.X),
                          reads=[("rst", 8)], writes=[("rst", 2)])
                    em.op("ve", lambda v: v.reciprocal(rst[:, 3:4], rst[:, 2:3]),
                          reads=[("rst", 2)], writes=[("rst", 3)])
                    em.op("ve", lambda v: v.tensor_scalar(lgt[:, 0:NE], lgt[:, NE:2 * NE],
                                                          rst[:, 15:16], None, ALU.is_ge),
                          reads=[("lgt", 1), ("rst", 8)], writes=[("lgt", 0)])
                    em.op("ve", lambda v: v.tensor_mul(lgt[:, 0:NE], lgt[:, 0:NE], lgt[:, NE:2 * NE]),
                          reads=[("lgt", 0), ("lgt", 1)], writes=[("lgt", 0)])
                    em.op("ve", lambda v, ttg=ttg: v.tensor_scalar_mul(
                        wde[:, ttg * NE:(ttg + 1) * NE], lgt[:, 0:NE], rst[:, 3:4]),
                        reads=[("lgt", 0), ("rst", 3)], writes=[("wde", ttg)])
            for e_i in range(NE):
                for half in range(2):
                    em.dma("pe", lambda e, e_i=e_i, half=half: e.dma_start(
                        out=wgu[half][:].rearrange("p (ct i) -> p ct i", ct=16),
                        in_=wgate[e_i][:, half * 384:(half + 1) * 384].rearrange("(ct p) i -> p ct i", p=128)),
                        "ffg", writes=[("wgu", half)])
                    em.close_batch("ffg")
                    em.dma("pe", lambda e, e_i=e_i, half=half: e.dma_start(
                        out=wgu[2 + half][:].rearrange("p (ct i) -> p ct i", ct=16),
                        in_=wup[e_i][:, half * 384:(half + 1) * 384].rearrange("(ct p) i -> p ct i", p=128)),
                        "ffu", writes=[("wgu", 2 + half)])
                    em.close_batch("ffu")
                for gno in range(GRPB):
                    tch, it = divmod(gno, 6)
                    pg = ps[gno % 2]
                    pu = ps[2 + gno % 2]
                    gh = it // 3
                    gi = it % 3
                    em.op("te", lambda t, tch=tch, gi=gi, gh=gh, pg=pg, gno=gno: [
                        t.matmul(pg[:, :],
                                 wgu[gh][:].rearrange("p (c i) -> p c i", c=16)[:, ct, gi * 128:(gi + 1) * 128],
                                 hfc[tch][:].rearrange("p (c t) -> p c t", c=16)[:, ct, :],
                                 start=(ct == 0), stop=(ct == 15)) for ct in range(16)][-1],
                        reads=[("wgu", gh), ("hfc", tch)], writes=[("psb", gno % 2)])
                    em.op("te", lambda t, tch=tch, gi=gi, gh=gh, pu=pu, gno=gno: [
                        t.matmul(pu[:, :],
                                 wgu[2 + gh][:].rearrange("p (c i) -> p c i", c=16)[:, ct, gi * 128:(gi + 1) * 128],
                                 hfc[tch][:].rearrange("p (c t) -> p c t", c=16)[:, ct, :],
                                 start=(ct == 0), stop=(ct == 15)) for ct in range(16)][-1],
                        reads=[("wgu", 2 + gh), ("hfc", tch)], writes=[("psb", 2 + gno % 2)])
                    em.op("se", lambda s_, gno=gno, pg=pg: s_.activation(
                        sgs[:, (gno % 2) * 512:(gno % 2) * 512 + 512], pg[:, :], AF.Silu),
                        reads=[("psb", gno % 2)], writes=[("sgs", gno % 2)])
                    em.op("ve", lambda v, gno=gno, tch=tch, it=it, pu=pu: v.tensor_tensor(
                        hexp[:].rearrange("p (i t) -> p i t", i=6)[:, it, tch * CH:(tch + 1) * CH],
                        sgs[:, (gno % 2) * 512:(gno % 2) * 512 + 512], pu[:, :], ALU.mult),
                        reads=[("sgs", gno % 2), ("psb", 2 + gno % 2)], writes=[("hexp", gno)])
                for fo in range(4):
                    wsd = fo % 2
                    em.dma("pe", lambda e, e_i=e_i, fo=fo, wsd=wsd: e.dma_start(
                        out=wdd[wsd][:].rearrange("p (i f) -> p i f", i=6),
                        in_=wdown[e_i][:, fo * 512:(fo + 1) * 512].rearrange("(i p) f -> p i f", p=128)),
                        "ffd", writes=[("wdd", wsd)])
                    em.close_batch("ffd")
                    for tt in range(NTTB):
                        pd = 4 + (fo * NTTB + tt) % 2
                        em.op("te", lambda t, tt=tt, fo=fo, wsd=wsd, pd=pd: [
                            t.matmul(ps[pd][:, :],
                                     hexp[:].rearrange("p (i t) -> p i t", i=6)[:, it_, tt * 128:(tt + 1) * 128],
                                     wdd[wsd][:].rearrange("p (i f) -> p i f", i=6)[:, it_, :],
                                     start=(it_ == 0), stop=(it_ == 5)) for it_ in range(6)][-1],
                            reads=[("wdd", wsd)] + [("hexp", tc_ * 6 + i_) for tc_ in range(NCHB) for i_ in range(6)],
                            writes=[("psb", pd)])
                        em.op("ve", lambda v, tt=tt, fo=fo, e_i=e_i, pd=pd: v.scalar_tensor_tensor(
                            accs[:].rearrange("p (tt f) -> p tt f", tt=NTTB)[:, tt, fo * 512:(fo + 1) * 512],
                            ps[pd][:, :], wde[:, tt * NE + e_i:tt * NE + e_i + 1],
                            accs[:].rearrange("p (tt f) -> p tt f", tt=NTTB)[:, tt, fo * 512:(fo + 1) * 512],
                            ALU.mult, ALU.add),
                            reads=[("psb", pd), ("wde", tt)], writes=[("accs",)])
            for tt in range(NTTB):
                trow = blk_i * TB + tt * 128
                em.dma("sy", lambda e, trow=trow: e.dma_start(
                    out=h1tD[:], in_=h1_s[trow:trow + 128, :]), "h1ld", reads=[("h1S",)], writes=[("h1tD",)])
                em.close_batch("h1ld")
                em.op("ve", lambda v, tt=tt: v.tensor_tensor(
                    h1tD[:], accs[:].rearrange("p (q f) -> p q f", q=NTTB)[:, tt, :],
                    h1tD[:], ALU.add),
                    reads=[("accs",), ("h1tD",)], writes=[("h1tD",)])
                em.dma("sy", lambda e, trow=trow: e.dma_start(
                    out=out_o[trow:trow + 128, :], in_=h1tD[:]), "outs", reads=[("h1tD",)])
                em.close_batch("outs")

    em.barrier()
    poolD.close()

    taps_f32 = {"ek": ek_s, "er": er_s, "h1": h1_s}
    taps_bf = {"q": q_s, "ktf": ktf_s, "ktt": ktt_s, "vtt": vtt_s,
               "rtf": rtf_s, "rrf": rrf_s, "y": y_s, "hf": hf_s}
    taps_keys = {"ek": ("ekS",), "er": ("ekS",), "h1": ("h1S",), "q": ("qS",),
                 "ktf": ("kfS",), "ktt": ("ktS",), "vtt": ("vtS",), "rtf": ("rfS",),
                 "rrf": ("rrS",), "y": ("yS",), "hf": ("hfS",)}
    for name in dbg:
        if name in taps_f32:
            em.dma("sy", lambda e, name=name: e.dma_start(out=dbg[name], in_=taps_f32[name]),
                   "dbgs", reads=[taps_keys[name]])
            em.close_batch("dbgs")
        elif name in taps_bf:
            em.dma("pe", lambda e, name=name: e.dma_start(out=dbg[name], in_=taps_bf[name]),
                   "dbgp", reads=[taps_keys[name]])
            em.close_batch("dbgp")
        elif name == "eTC":
            em.dma("sy", lambda e: e.dma_start(out=dbg["eTC"], in_=eTC[:]), "dbgs",
                   reads=[("eTC", c0x) for c0x in range(0, 16 * NM, MPC)])
            em.close_batch("dbgs")
        elif name == "eL":
            em.dma("sy", lambda e: e.dma_start(out=dbg["eL"], in_=eLs[:]), "dbgs",
                   reads=[("eL", c0x) for c0x in range(0, 16 * NM, MPC)])
            em.close_batch("dbgs")
        elif name == "Dsum":
            em.dma("sy", lambda e: e.dma_start(out=dbg["Dsum"], in_=DsumB[:]), "dbgs",
                   reads=[("Dsum",)])
            em.close_batch("dbgs")
        elif name == "Spack":
            em.dma("sy", lambda e: e.dma_start(out=dbg["Spack"], in_=SpackB[:]), "dbgs",
                   reads=[("Spack",)])
            em.close_batch("dbgs")
        elif name == "Sin":
            em.dma("sy", lambda e: e.dma_start(out=dbg["Sin"], in_=SinB[:]), "dbgs",
                   reads=[("Sin",)])
            em.close_batch("dbgs")
    em.close_batch("dbgs")

    fs = []
    if "K" in phases:
        fs.append("outs")
    if dbg:
        fs += ["dbgs", "dbgp"]
    em.emit(nc, stack, final_stream=fs)
    return nc, out_names, stack, em



def _host_consts(rank):
    ident = np.eye(128, dtype=np.float32)
    c = np.zeros((32, 32), np.float32)
    for a in range(32):
        c[a, a:] = 1.0
    mrepb = np.tile(c, (4, 16)).astype(np.float32)
    mseg = np.ones((128, CH), np.float32)
    mseg[:, 0::L] = 0.0
    gmask = np.zeros((128, NCORE), np.float32)
    gmask[:, :rank] = 1.0
    return ident, mrepb, mseg, gmask


def kernel(L0_attn_x, L0_attn_Wx, L0_attn_Ww, L0_attn_bw, L0_attn_Wk, L0_attn_Wv,
           L0_attn_Wr, L0_attn_Wo, L0_attn_gamma, L0_attn_beta, L0_ffn_router_w,
           L0_ffn_w_gate, L0_ffn_w_up, L0_ffn_w_down):
    T = L0_attn_x.shape[1] // NCORE
    x = np.ascontiguousarray(np.asarray(L0_attn_x, dtype=np.float32).reshape(NCORE * T, D))
    f32 = lambda a: np.ascontiguousarray(np.asarray(a, dtype=np.float32))
    base = dict(
        Wx=f32(L0_attn_Wx), Ww=f32(L0_attn_Ww),
        bw=np.ascontiguousarray(f32(L0_attn_bw).reshape(16, 128).T),
        Wk=f32(L0_attn_Wk), Wv=f32(L0_attn_Wv), Wr=f32(L0_attn_Wr), Wo=f32(L0_attn_Wo),
        gamma=np.ascontiguousarray(f32(L0_attn_gamma).reshape(16, 128).T),
        beta=np.ascontiguousarray(f32(L0_attn_beta).reshape(16, 128).T),
        rw=f32(L0_ffn_router_w), wg=f32(L0_ffn_w_gate), wu=f32(L0_ffn_w_up),
        wdn=f32(L0_ffn_w_down),
    )
    nc, out_names, stack, em = build_program(T=T)
    in_maps = []
    for r in range(NCORE):
        ident, mrepb, mseg, gmask = _host_consts(r)
        m = dict(base)
        m["x"] = x[r * T:(r + 1) * T]
        m["identity"] = ident
        m["mrepb_c"] = mrepb
        m["mseg_c"] = mseg
        m["gmask_c"] = gmask
        in_maps.append(m)
    res = run_bass_kernel_spmd(nc, in_maps, core_ids=list(range(NCORE)))
    stack.close()
    out = np.concatenate([np.asarray(res.results[r]["out"]) for r in range(NCORE)], axis=0)
    return out.reshape(1, NCORE * T, D).astype(np.float32)
